# version 17
# baseline (speedup 1.0000x reference)
import math
import numpy as np
import ml_dtypes

import concourse.bass as bass
import concourse.mybir as mybir
from concourse.bass_utils import run_bass_kernel_spmd

F32 = mybir.dt.float32
BF16 = mybir.dt.bfloat16
AF = mybir.ActivationFunctionType
ALU = mybir.AluOpType
AX = mybir.AxisListType
BF = ml_dtypes.bfloat16

P = 128
NEG = 1.0e30


def full_cfg():
    return dict(S=2048, D=2048, F=5632, HD=128, NKV=4, theta=1e4,
                TT=512, ncores=8)


def small_cfg():
    return dict(S=1024, D=512, F=1024, HD=128, NKV=1, theta=1e4,
                TT=512, ncores=2)


class Prog:
    ENGINES = ("te", "act", "dve", "pool", "sync")
    QUEUES = ("sync", "pool")

    def __init__(self, nc):
        self.nc = nc
        self.sem = {k: nc.alloc_semaphore("sem_" + k) for k in self.ENGINES}
        self.dsem = {}
        self.cnt = {k: 0 for k in self.ENGINES}
        self.dcnt = {}
        self.ops = {k: [] for k in self.ENGINES}

    def _norm(self, deps):
        out = []
        for d in deps:
            if d is None:
                continue
            if isinstance(d, list):
                out.extend(t for t in d if t is not None)
            else:
                out.append(d)
        dd = {}
        for (s, v) in out:
            if v <= 0:
                continue
            k = id(s)
            if k not in dd or dd[k][1] < v:
                dd[k] = (s, v)
        return list(dd.values())

    def _emit(self, key, fn, deps, inc_sem, inc_n):
        waits = self._norm(deps)

        def run(eng):
            for (s, v) in waits:
                eng.wait_ge(s, v)
            inst = fn(eng)
            if inc_sem is not None:
                inst.then_inc(inc_sem, inc_n)
        self.ops[key].append(run)

    def op(self, key, fn, deps=()):
        deps = list(deps)
        if key in ("act", "dve", "pool") and self.cnt[key] > 0:
            deps.append((self.sem[key], self.cnt[key]))
        self._emit(key, fn, deps, self.sem[key], 1)
        self.cnt[key] += 1
        return (self.sem[key], self.cnt[key])

    def te(self, fn, deps=()):
        self._emit("te", fn, deps, self.sem["te"], 1)
        self.cnt["te"] += 1
        return (self.sem["te"], self.cnt["te"])

    def dma(self, queue, fn, deps=(), key=None):
        key = key or queue
        if key not in self.dsem:
            self.dsem[key] = self.nc.alloc_semaphore("dsem_" + key)
            self.dcnt[key] = 0
        self._emit(queue, fn, deps, self.dsem[key], 16)
        self.dcnt[key] += 16
        return (self.dsem[key], self.dcnt[key])

    def dma_join(self):
        return [(self.dsem[k], self.dcnt[k]) for k in self.dsem]

    def barrier_wait(self, key, deps):
        self._emit(key, lambda e: None, deps, None, 0)

    def join(self):
        return [(self.sem[k], self.cnt[k]) for k in ("te", "act", "dve")]

    def finalize(self):
        nc = self.nc
        with nc.Block() as blk:
            @blk.tensor
            def _(e):
                for f in self.ops["te"]:
                    f(e)

            @blk.scalar
            def _(e):
                for f in self.ops["act"]:
                    f(e)

            @blk.vector
            def _(e):
                for f in self.ops["dve"]:
                    f(e)

            @blk.gpsimd
            def _(e):
                for f in self.ops["pool"]:
                    f(e)

            @blk.sync
            def _(e):
                for f in self.ops["sync"]:
                    f(e)


def build_program(cfg, debug=False):
    S, D, F, HD = cfg["S"], cfg["D"], cfg["F"], cfg["HD"]
    NKV, TT = cfg["NKV"], cfg["TT"]
    ncores = cfg["ncores"]
    NH = D // HD
    KVD = NKV * HD
    GROUPS = NH // NKV
    DC = D // P
    FC = F // P
    KC = KVD // P
    T128 = S // P
    NTT = S // TT
    TSUB = TT // P
    NW = F * D
    SHW = NW // P
    INVSQ = 1.0 / math.sqrt(HD)
    W3CG = None
    for c in range(2048, 0, -128):
        if F % c == 0:
            W3CG = c
            break
    NW3G = F // W3CG
    PCG = max(min(D, 2048), W3CG)

    nc = bass.Bass("TRN2", target_bir_lowering=False, debug=False)

    x_in = nc.dram_tensor("x", [S, D], F32, kind="ExternalInput").ap()
    wq_in = nc.dram_tensor("wq", [D, D], F32, kind="ExternalInput").ap()
    wk_in = nc.dram_tensor("wk", [KVD, D], F32, kind="ExternalInput").ap()
    wv_in = nc.dram_tensor("wv", [KVD, D], F32, kind="ExternalInput").ap()
    wo_in = nc.dram_tensor("wo", [D, D], F32, kind="ExternalInput").ap()
    w1_in = nc.dram_tensor("w1", [F, D], F32, kind="ExternalInput").ap()
    w2_in = nc.dram_tensor("w2", [F, D], F32, kind="ExternalInput").ap()
    w3_in = nc.dram_tensor("w3", [D, F], F32, kind="ExternalInput").ap()
    gsh1 = nc.dram_tensor("gsh1", [P, SHW], F32, kind="ExternalInput").ap()
    gsh2 = nc.dram_tensor("gsh2", [P, SHW], F32, kind="ExternalInput").ap()
    gsh3 = nc.dram_tensor("gsh3", [P, SHW], F32, kind="ExternalInput").ap()
    cid = nc.dram_tensor("cid", [P, P], BF16, kind="ExternalInput").ap()
    ctm = nc.dram_tensor("ctm", [P, P], BF16, kind="ExternalInput").ap()
    cpm = nc.dram_tensor("cpm", [P, P], BF16, kind="ExternalInput").ap()
    cropec = nc.dram_tensor("cropec", [P, S], BF16, kind="ExternalInput").ap()
    cropes = nc.dram_tensor("cropes", [P, S], BF16, kind="ExternalInput").ap()
    cmisc = nc.dram_tensor("cmisc", [P, 4], F32, kind="ExternalInput").ap()
    conesr = nc.dram_tensor("conesr", [1, P], F32, kind="ExternalInput").ap()
    y_out = nc.dram_tensor("y", [S, D], F32, kind="ExternalOutput").ap()
    if debug:
        dbg1 = nc.dram_tensor("dbg1", [P, T128 * D], BF16, kind="ExternalOutput").ap()
        dbg2 = nc.dram_tensor("dbg2", [P, T128 * D], BF16, kind="ExternalOutput").ap()
        dbgt = nc.dram_tensor("dbgt", [P, 8], F32, kind="ExternalOutput").ap()

    QW1 = nc.dram_tensor("QW1", [F, D], BF16).ap()
    QW2 = nc.dram_tensor("QW2", [F, D], BF16).ap()
    QW3 = nc.dram_tensor("QW3", [D, F], BF16).ap()
    AWQ = nc.dram_tensor("AWQ", [D, D], BF16).ap()
    AWK = nc.dram_tensor("AWK", [KVD, D], BF16).ap()
    AWV = nc.dram_tensor("AWV", [KVD, D], BF16).ap()
    AWO = nc.dram_tensor("AWO", [D, D], BF16).ap()

    ffn_sz = DC * TT + FC * TT + 2048 + 2 * D + 2 * D + 2 * F
    o_ak = 0
    o_av = o_ak + KC * S
    o_sa = o_av + T128 * KVD
    o_sb = o_sa + DC * TT
    sz_sb = max(T128 * TT, DC * TT, KC * D)
    o_sc = o_sb + sz_sb
    sz_sc = max(DC * TT, KC * D)
    o_ab = o_sc + sz_sc
    o_ks = o_ab + 2 * S
    o_hq = o_ks + 2 * TT
    attn_sz = o_hq + D
    prep_sz = 7 * PCG + 2048
    NBF = max(ffn_sz, attn_sz, prep_sz)

    R = nc.alloc_sbuf_tensor("R", [P, T128 * D], BF16)
    ARB = nc.alloc_sbuf_tensor("ARB", [P, NBF], BF16)
    ARF = nc.alloc_sbuf_tensor("ARF", [P, 3 * 2048 + 2 * 512], F32)
    idn = nc.alloc_sbuf_tensor("idn", [P, P], BF16)
    tmk = nc.alloc_sbuf_tensor("tmk", [P, P], BF16)
    pmt = nc.alloc_sbuf_tensor("pmt", [P, P], BF16)
    ropec = nc.alloc_sbuf_tensor("ropec", [P, S], BF16)
    ropes = nc.alloc_sbuf_tensor("ropes", [P, S], BF16)
    misc = nc.alloc_sbuf_tensor("misc", [P, 4], F32)
    onesr = nc.alloc_sbuf_tensor("onesr", [1, P], F32)
    stats = nc.alloc_sbuf_tensor("stats", [P, 24], F32)
    mv = nc.alloc_sbuf_tensor("mv", [P, 2], F32)
    rstd = nc.alloc_sbuf_tensor("rstd", [P, 1], F32)
    ngch_max = 3 * ((SHW + 2047) // 2048)
    gpart = nc.alloc_sbuf_tensor("gpart", [P, max(24, ngch_max)], F32)
    g3 = nc.alloc_sbuf_tensor("g3", [P, 4], F32)
    gsb = nc.alloc_sbuf_tensor("gsb", [1, 4], F32)
    gsum2 = nc.alloc_sbuf_tensor("gsum2", [1, 4], F32)
    thr = nc.alloc_sbuf_tensor("thr", [P, 8], F32)
    mx = nc.alloc_sbuf_tensor("mx", [P, 1], F32)
    sums = nc.alloc_sbuf_tensor("sums", [P, 1], F32)
    rs = nc.alloc_sbuf_tensor("rs", [P, 1], F32)
    mxp2 = nc.alloc_sbuf_tensor("mxp2", [P, 2], F32)
    mxn2 = nc.alloc_sbuf_tensor("mxn2", [P, 4], F32)
    sums2 = nc.alloc_sbuf_tensor("sums2", [P, 4], F32)
    rs2 = nc.alloc_sbuf_tensor("rs2", [P, 4], F32)

    PSF = nc.alloc_psum_tensor("PSF", [P, 2048], F32)
    PSG = nc.alloc_psum_tensor("PSG", [P, 1024], F32)
    PSB = nc.alloc_psum_tensor("PSB", [P, 1024], BF16)
    PSW = nc.alloc_psum_tensor("PSW", [P, 1024], BF16)

    WS = [ARF[:, 0:2048], ARF[:, 2048:4096], ARF[:, 4096:6144]]
    SC1 = ARF[:, 6144:6656]
    SC2 = ARF[:, 6656:7168]
    OUTS = [SC1, SC2]
    SSB = [ARF[:, 0:2048], ARF[:, 2048:4096],
           ARF[:, 4096:6144]]
    R3 = R.rearrange("p (t d) -> p t d", t=T128)
    Y4 = y_out.rearrange("(b p) d -> p b d", p=P)

    eps_ap = misc[:, 0:1]
    ones1 = misc[:, 1:2]

    p = Prog(nc)

    tcs = None
    for dst, src in ((idn, cid), (tmk, ctm), (pmt, cpm), (ropec, cropec),
                     (ropes, cropes), (misc, cmisc), (onesr, conesr)):
        tcs = p.dma("sync", lambda e, d=dst, s=src: e.dma_start(out=d[:], in_=s[:, :]),
                    key="consts")
    t_consts = tcs

    gchunks = []
    off = 0
    while off < SHW:
        c = min(2048, SHW - off)
        gchunks.append((off, c))
        off += c
    ngch = len(gchunks)

    ws_tok = [None, None, None]
    ws_ctr = [0]
    gs_tok = [None, None]
    gs_ctr = [0]
    dump_junk = ARB[:, 7 * PCG:7 * PCG + 2048]
    qn_tok = [None, None, None]
    tt_tok = [None, None, None, None]
    tt_ctr = [0]
    psw_state = [None, None]
    pvb_ctr = [0]
    dump_tok = {}
    gp_state = {"idx": 0, "last": None}

    g_steps = []
    for m, gsh in enumerate((gsh1, gsh2, gsh3)):
        for (o, c) in gchunks:
            def step(m=m, gsh=gsh, o=o, c=c):
                sl = ws_ctr[0] % 3
                ws_ctr[0] += 1
                k = gp_state["idx"]
                gp_state["idx"] += 1
                t_l = p.dma("sync", lambda e, sl=sl, g=gsh, o=o, c=c:
                            e.dma_start(out=WS[sl][:, 0:c], in_=g[:, o:o + c]),
                            deps=[t_consts, ws_tok[sl]], key="ws%d" % sl)
                t_abs = p.op("act", lambda e, sl=sl, c=c, k=k:
                             e.activation(out=dump_junk[:, 0:c],
                                          in_=WS[sl][:, 0:c],
                                          func=AF.Abs,
                                          accum_out=gpart[:, k:k + 1]),
                             deps=[t_l])
                ws_tok[sl] = t_abs
                gp_state["last"] = t_abs
            g_steps.append(step)

    QNAT = [ARB[:, j * PCG:(j + 1) * PCG] for j in range(3)]
    TTILE = [ARB[:, (3 + i) * PCG:(4 + i) * PCG] for i in range(4)]

    def prep_steps(name, src_ap, dst_ap, rows, cols, CG, quant, thr_idx, dep0f,
                   QN=None, TTL=None, qn_t=None, tt_t=None, banks=(0, 1),
                   dkey="dump"):
        QN = QN if QN is not None else QNAT
        TTL = TTL if TTL is not None else TTILE
        qn_t = qn_t if qn_t is not None else qn_tok
        tt_t = tt_t if tt_t is not None else tt_tok
        nqn = len(QN)
        ntt = len(TTL)
        NRC = rows // P
        NCGr = cols // CG
        NCB = CG // P
        steps = []
        for rc in range(NRC):
            for cg in range(NCGr):
                def step(rc=rc, cg=cg):
                    i = ws_ctr[0] % nqn
                    it = tt_ctr[0] % ntt
                    tt_ctr[0] += 1
                    ws_ctr[0] += 1
                    t_l = p.dma("sync", lambda e, i=i, rc=rc, cg=cg:
                                e.dma_start(out=WS[i % 3][:, 0:CG],
                                            in_=src_ap[rc * P:(rc + 1) * P,
                                                       cg * CG:(cg + 1) * CG]),
                                deps=[dep0f(), ws_tok[i % 3]], key="ws%d" % (i % 3))
                    if quant:
                        p.op("dve", lambda e, i=i: e.tensor_scalar(
                            out=QN[i][:, 0:CG], in0=WS[i % 3][:, 0:CG],
                            scalar1=thr[:, thr_idx:thr_idx + 1], scalar2=None,
                            op0=ALU.is_ge), deps=[t_l, qn_t[i]])
                        t_q = p.op("dve", lambda e, i=i: e.scalar_tensor_tensor(
                            out=QN[i][:, 0:CG], in0=WS[i % 3][:, 0:CG],
                            scalar=thr[:, 4 + thr_idx:5 + thr_idx],
                            in1=QN[i][:, 0:CG], op0=ALU.is_le,
                            op1=ALU.subtract))
                    else:
                        t_q = p.op("dve", lambda e, i=i: e.tensor_copy(
                            out=QN[i][:, 0:CG], in_=WS[i % 3][:, 0:CG]),
                            deps=[t_l, qn_t[i]])
                    ws_tok[i % 3] = t_q
                    evs = []
                    first = True
                    for r0 in range(0, NCB, 8):
                        rn = min(8, NCB - r0)
                        bk = banks[pvb_ctr[0] % len(banks)]
                        pvb_ctr[0] += 1
                        PSX = PSW if bk == 0 else PSB
                        tdeps = [t_q, t_consts, psw_state[bk]]
                        if first:
                            tdeps.append(tt_t[it])
                            first = False
                        t_t = p.te(lambda e, i=i, r0=r0, rn=rn, PSX=PSX: [
                            e.transpose(PSX[:, k * 128:(k + 1) * 128],
                                        QN[i][:, (r0 + k) * 128:(r0 + k + 1) * 128],
                                        idn[:])
                            for k in range(rn)][-1], deps=tdeps)

                        def mk_pe(e, it=it, r0=r0, rn=rn, PSX=PSX, bk=bk):
                            dst = TTL[it][:, r0 * 128:(r0 + rn) * 128]
                            if bk == 0:
                                return e.activation(out=dst, in_=PSX[:, 0:rn * 128],
                                                    func=AF.Copy)
                            return e.tensor_copy(out=dst, in_=PSX[:, 0:rn * 128])
                        ev = p.op("act" if bk == 0 else "dve", mk_pe, deps=[t_t])
                        psw_state[bk] = ev
                        evs.append(ev)
                    qn_t[i] = evs
                    t_d = p.dma("pool", lambda e, it=it, rc=rc, cg=cg:
                                e.dma_start(
                                    out=dst_ap[rc * P:(rc + 1) * P,
                                               cg * CG:(cg + 1) * CG],
                                    in_=TTL[it][:, 0:CG]),
                                deps=evs, key="%s%d" % (dkey, it))
                    tt_t[it] = t_d
                    dump_tok[(name, rc, cg)] = t_d
                steps.append(step)
        return steps

    CGD = min(D, 2048)
    dep_c = lambda: t_consts
    r_tok = [None] * T128
    x_steps = []
    for t in range(T128):
        def xstep(t=t):
            sl = ws_ctr[0] % 3
            ws_ctr[0] += 1
            t_l = p.dma("sync", lambda e, sl=sl, t=t:
                        e.dma_start(out=WS[sl][:, 0:D],
                                    in_=x_in[t * P:(t + 1) * P, :]),
                        deps=[t_consts, ws_tok[sl]], key="ws%d" % sl)
            t_c = p.op("dve", lambda e, sl=sl, t=t:
                       e.tensor_copy(out=R[:, t * D:(t + 1) * D],
                                     in_=WS[sl][:, 0:D]),
                       deps=[t_l])
            ws_tok[sl] = t_c
            r_tok[t] = t_c
        x_steps.append(xstep)
    a_steps = (prep_steps("wq", wq_in, AWQ, D, D, CGD, False, 0, dep_c)
               + prep_steps("wk", wk_in, AWK, KVD, D, CGD, False, 0, dep_c)
               + prep_steps("wv", wv_in, AWV, KVD, D, CGD, False, 0, dep_c)
               + prep_steps("wo", wo_in, AWO, D, D, CGD, False, 0, dep_c)
               + x_steps)
    na, ng_ = len(a_steps), len(g_steps)
    ai = 0
    for gi, gs in enumerate(g_steps):
        gs()
        while ai * ng_ < (gi + 1) * na:
            a_steps[ai]()
            ai += 1
    while ai < na:
        a_steps[ai]()
        ai += 1
    t_dumps = [(p.dsem["dump%d" % i], p.dcnt["dump%d" % i]) for i in range(4)]

    t_abs = gp_state["last"]
    t_red = None
    for m in range(3):
        t_red = p.op("dve", lambda e, m=m: e.tensor_reduce(
            out=g3[:, m:m + 1], in_=gpart[:, m * ngch:(m + 1) * ngch],
            axis=AX.X, op=ALU.add), deps=[t_abs])
    t_z = p.op("dve", lambda e: e.memset(g3[:, 3:4], 0.0), deps=[t_red])
    t_gmm = p.te(lambda e: e.matmul(PSG[0:1, 0:4], ones1, g3[:, 0:4],
                                    start=True, stop=True), deps=[t_z, t_consts])
    t_gc = p.op("dve", lambda e: e.tensor_copy(out=gsb[:, :], in_=PSG[0:1, 0:4]),
                deps=[t_gmm])
    t_bc = p.te(lambda e: e.matmul(PSG[:, 0:4], onesr[:, :], gsb[:, :],
                                   start=True, stop=True), deps=[t_gc])
    p.op("act", lambda e: e.activation(
        out=thr[:, 0:4], in_=PSG[:, 0:4], func=AF.Copy,
        scale=0.5 / NW, bias=0.5e-8), deps=[t_bc])
    t_thr = p.op("act", lambda e: e.activation(
        out=thr[:, 4:8], in_=PSG[:, 0:4], func=AF.Copy,
        scale=-0.5 / NW, bias=-0.5e-8), deps=[t_bc])
    if debug:
        p.dma("pool", lambda e: e.dma_start(out=dbgt[:, :], in_=thr[:, :]),
              deps=[t_thr], key="dbg")

    nch = D // 512

    def emit_ln(r_off, out_ap, deps):
        d = list(deps)
        for c in range(nch):
            p.op("dve", lambda e, c=c: e.bn_stats(
                out=stats[:, 6 * c:6 * (c + 1)],
                in_=R[:, r_off + c * 512:r_off + (c + 1) * 512]), deps=d)
            d = []
        t_ag = p.op("dve", lambda e: e.bn_aggr(out=mv[:, :], in_=stats[:, 0:6 * nch]))
        t_s = p.op("act", lambda e: e.activation(
            out=rstd[:, :], in_=mv[:, 1:2], func=AF.Sqrt, bias=eps_ap),
            deps=[t_ag, t_consts])
        p.op("dve", lambda e: e.reciprocal(out=rstd[:, :], in_=rstd[:, :]),
             deps=[t_s])
        t_n = p.op("dve", lambda e: e.tensor_scalar(
            out=out_ap, in0=R[:, r_off:r_off + D],
            scalar1=mv[:, 0:1], scalar2=rstd[:, :],
            op0=ALU.subtract, op1=ALU.mult))
        return t_n

    HFM = ARB[:, 0:DC * TT]
    HFM3 = HFM.rearrange("p (c t) -> p c t", c=DC)
    HIDDEN = ARB[:, DC * TT:DC * TT + FC * TT]
    HSTG = ARB[:, DC * TT + FC * TT:DC * TT + FC * TT + 2048]
    WAR0 = DC * TT + FC * TT + 2048
    W1S = [ARB[:, WAR0 + k * D:WAR0 + (k + 1) * D] for k in range(2)]
    W2S = [ARB[:, WAR0 + 2 * D + k * D:WAR0 + 2 * D + (k + 1) * D] for k in range(2)]
    W3S = [ARB[:, WAR0 + 4 * D + k * F:WAR0 + 4 * D + (k + 1) * F] for k in range(2)]

    def emit_ffn(final, phase_dep, w3dep=None, out_box=None, direct=None,
                 psbox=None):
        w1_tok = [None, None]
        w2_tok = [None, None]
        w3_tok = [None, None]
        if psbox is None:
            psbox = [phase_dep]
        psfq = [phase_dep] * 4
        sig_tok = [None, None]
        hfm_te_tok = None
        hid_war = None
        outs_tok = [phase_dep, phase_dep]
        last_out = None

        for tt in range(NTT):
            t_hfm_ev = []
            for u in range(TSUB):
                t = tt * TSUB + u
                w3w = direct["w3war"]() if (direct and tt >= 1) else None
                t_n = emit_ln(t * D, HSTG[:, 0:D],
                              [r_tok[t], phase_dep, hfm_te_tok, w3w])
                for k0 in range(0, DC, 8):
                    kn = min(8, DC - k0)
                    t_t = p.te(lambda e, k0=k0, kn=kn: [
                        e.transpose(PSB[:, k * 128:(k + 1) * 128],
                                    HSTG[:, (k0 + k) * 128:(k0 + k + 1) * 128],
                                    idn[:]) for k in range(kn)][-1],
                        deps=[t_n, psbox[0], t_consts])
                    hfm_te_tok = t_t
                    t_e = p.op("act", lambda e, k0=k0, kn=kn, u=u: e.activation(
                        out=HFM3[:, k0:k0 + kn, u * 128:(u + 1) * 128],
                        in_=PSB[:, 0:kn * 128], func=AF.Copy), deps=[t_t])
                    psbox[0] = t_e
                    t_hfm_ev.append(t_e)
            t_hid = {}
            for ft in range(FC):
                yield ("ft", tt, ft)
                sl = ft % 2
                if direct is not None and tt == 0:
                    W1A, W2A = W1S[sl], W2S[sl]
                    d1 = list(direct["evs"][("w1", ft)])
                    d2 = list(direct["evs"][("w2", ft)])
                else:
                    if direct is not None and not direct.get("seeded"):
                        w1_tok[0], w1_tok[1] = direct["war1"][0], direct["war1"][1]
                        w2_tok[0], w2_tok[1] = direct["war2"][0], direct["war2"][1]
                        direct["seeded"] = True
                    t_l1 = p.dma("pool", lambda e, sl=sl, ft=ft:
                                 e.dma_start(out=W1S[sl][:, :],
                                             in_=QW1[ft * P:(ft + 1) * P, :]),
                                 deps=[dump_tok[("w1", ft, 0)], w1_tok[sl],
                                       phase_dep], key="w1s%d" % sl)
                    t_l2 = p.dma("pool", lambda e, sl=sl, ft=ft:
                                 e.dma_start(out=W2S[sl][:, :],
                                             in_=QW2[ft * P:(ft + 1) * P, :]),
                                 deps=[dump_tok[("w2", ft, 0)], w2_tok[sl],
                                       phase_dep], key="w2s%d" % sl)
                    W1A, W2A = W1S[sl], W2S[sl]
                    d1, d2 = [t_l1], [t_l2]
                bs = ft % 2
                q1 = bs * 1024
                q2 = bs * 1024 + 512
                t_m1 = p.te(lambda e, W=W1A, q1=q1: [
                    e.matmul(PSF[:, q1 + 0:q1 + TT],
                             W[:, k * 128:(k + 1) * 128],
                             HFM[:, k * TT:(k + 1) * TT],
                             start=(k == 0), stop=(k == DC - 1))
                    for k in range(DC)][-1],
                    deps=t_hfm_ev + d1 + [psfq[2 * bs]])
                t_m2 = p.te(lambda e, W=W2A, q2=q2: [
                    e.matmul(PSF[:, q2:q2 + TT],
                             W[:, k * 128:(k + 1) * 128],
                             HFM[:, k * TT:(k + 1) * TT],
                             start=(k == 0), stop=(k == DC - 1))
                    for k in range(DC)][-1], deps=d2 + [psfq[2 * bs + 1]])
                if direct is not None and tt == 0:
                    direct["feed"](ft, t_m1, t_m2)
                else:
                    w1_tok[sl] = t_m1
                    w2_tok[sl] = t_m2
                hfm_te_tok = t_m2
                t_sg = p.op("act", lambda e, sl=sl, q1=q1: e.activation(
                    out=HSTG[:, sl * 512:sl * 512 + 512], in_=PSF[:, q1:q1 + TT],
                    func=AF.Sigmoid, scale=-1.0), deps=[t_m1, sig_tok[sl]])
                psfq[2 * bs] = t_sg
                t_h = p.op("dve", lambda e, sl=sl, q2=q2, ft=ft: e.tensor_tensor(
                    out=HIDDEN[:, ft * TT:(ft + 1) * TT],
                    in0=HSTG[:, sl * 512:sl * 512 + 512],
                    in1=PSF[:, q2:q2 + TT], op=ALU.mult),
                    deps=[t_sg, t_m2, hid_war])
                psfq[2 * bs + 1] = t_h
                sig_tok[sl] = t_h
                t_hid[ft] = t_h
            t_hid_all = t_hid[FC - 1]
            for dt in range(DC):
                yield ("dt", tt, dt)
                sl = dt % 2
                t_l3 = p.dma("pool", lambda e, sl=sl, dt=dt:
                             e.dma_start(out=W3S[sl][:, :],
                                         in_=QW3[dt * P:(dt + 1) * P, :]),
                             deps=[dump_tok[("w3", dt, cg)]
                                   for cg in range(NW3G)]
                             + [w3_tok[sl], phase_dep,
                                (w3dep[0] if w3dep else None)],
                             key="w3s%d" % sl)
                bs = dt % 4
                q3 = bs * 512
                t_m3 = p.te(lambda e, sl=sl, q3=q3: [
                    e.matmul(PSF[:, q3:q3 + TT],
                             W3S[sl][:, hc * 128:(hc + 1) * 128],
                             HIDDEN[:, hc * TT:(hc + 1) * TT],
                             start=(hc == 0), stop=(hc == FC - 1))
                    for hc in range(FC)][-1],
                    deps=[t_l3, t_hid_all, psfq[bs]])
                w3_tok[sl] = t_m3
                hid_war = t_m3
                xsl = dt % 2
                t_e3 = p.op("act", lambda e, q3=q3, xsl=xsl: e.activation(
                    out=HSTG[:, 1024 + xsl * 512:1536 + xsl * 512],
                    in_=PSF[:, q3:q3 + TT], func=AF.Copy),
                    deps=[t_m3])
                psfq[bs] = t_e3
                t_t = p.te(lambda e, xsl=xsl: [
                    e.transpose(PSB[:, u * 128:(u + 1) * 128],
                                HSTG[:, 1024 + xsl * 512 + u * 128:
                                      1024 + xsl * 512 + (u + 1) * 128],
                                idn[:]) for u in range(TSUB)][-1],
                    deps=[t_e3, psbox[0]])
                hfm_te_tok = t_t
                if not final:
                    t_r = p.op("dve", lambda e, tt=tt, dt=dt:
                               e.scalar_tensor_tensor(
                                   out=R3[:, tt * TSUB:(tt + 1) * TSUB,
                                          dt * 128:(dt + 1) * 128],
                                   in0=PSB[:, 0:TSUB * 128], scalar=0.5,
                                   in1=R3[:, tt * TSUB:(tt + 1) * TSUB,
                                          dt * 128:(dt + 1) * 128],
                                   op0=ALU.mult, op1=ALU.add), deps=[t_t])
                    for u in range(TSUB):
                        r_tok[tt * TSUB + u] = t_r
                    psbox[0] = t_r
                else:
                    osl = dt % 2
                    t_o = p.op("dve", lambda e, tt=tt, dt=dt, osl=osl:
                               e.scalar_tensor_tensor(
                                   out=OUTS[osl][:, 0:TSUB * 128],
                                   in0=PSB[:, 0:TSUB * 128], scalar=0.5,
                                   in1=R3[:, tt * TSUB:(tt + 1) * TSUB,
                                          dt * 128:(dt + 1) * 128],
                                   op0=ALU.mult, op1=ALU.add),
                               deps=[t_t, outs_tok[osl]])
                    t_w = p.dma("sync", lambda e, tt=tt, dt=dt, osl=osl:
                                e.dma_start(
                                    out=Y4[:, tt * TSUB:(tt + 1) * TSUB,
                                           dt * 128:(dt + 1) * 128],
                                    in_=OUTS[osl][:, 0:TSUB * 128]), deps=[t_o],
                                key="outs%d" % osl)
                    outs_tok[osl] = t_w
                    psbox[0] = t_o
                    last_out = t_w
        if out_box is not None:
            out_box[0] = last_out

    GB = WAR0 + 4 * D
    QN_B = [ARB[:, GB + j * CGD:GB + (j + 1) * CGD] for j in range(4)]
    assert 4 * CGD <= 2 * F
    war_w1 = [[], []]
    war_w2 = [[], []]
    evs_direct = {}
    qnb_tok = [None, None, None, None]
    qld_tok = {}
    psbox = [psw_state[1]]

    def emit_pair_lq(ft):
        for (name, src, qeng, qn) in (("w1", w1_in, "dve", (2 * ft) % 4),
                                      ("w2", w2_in, "dve", (2 * ft + 1) % 4)):
            ti = 0 if name == "w1" else 1
            sl = ws_ctr[0] % 3
            ws_ctr[0] += 1
            t_l = p.dma("sync", lambda e, sl=sl, src=src, ft=ft:
                        e.dma_start(out=WS[sl][:, 0:D],
                                    in_=src[ft * P:(ft + 1) * P, :]),
                        deps=[t_thr, ws_tok[sl]], key="ws%d" % sl)
            p.op(qeng, lambda e, qn=qn, sl=sl, ti=ti: e.tensor_scalar(
                out=QN_B[qn][:, 0:D], in0=WS[sl][:, 0:D],
                scalar1=thr[:, ti:ti + 1], scalar2=None,
                op0=ALU.is_ge), deps=[t_l, qnb_tok[qn]])
            t_q = p.op(qeng, lambda e, qn=qn, sl=sl, ti=ti: e.scalar_tensor_tensor(
                out=QN_B[qn][:, 0:D], in0=WS[sl][:, 0:D],
                scalar=thr[:, 4 + ti:5 + ti], in1=QN_B[qn][:, 0:D],
                op0=ALU.is_le, op1=ALU.subtract))
            ws_tok[sl] = t_q
            qld_tok[(name, ft)] = (qn, t_q)

    def emit_pair_tx(ft):
        sl2 = ft % 2
        for (name, dstQW, WSL) in (("w1", QW1, W1S), ("w2", QW2, W2S)):
            war = war_w1 if name == "w1" else war_w2
            qn, t_q = qld_tok.pop((name, ft))
            evs = []
            for ri, r0 in enumerate(range(0, DC, 8)):
                rn = min(8, DC - r0)
                PSX = PSW if ri % 2 == 0 else PSB
                pstok = psw_state if ri % 2 == 0 else psbox
                t_t = p.te(lambda e, qn=qn, r0=r0, rn=rn, PSX=PSX: [
                    e.transpose(PSX[:, k * 128:(k + 1) * 128],
                                QN_B[qn][:, (r0 + k) * 128:(r0 + k + 1) * 128],
                                idn[:]) for k in range(rn)][-1],
                    deps=[t_q, t_consts, pstok[0]])
                ev = p.op("act", lambda e, W=WSL[sl2], r0=r0, rn=rn, PSX=PSX:
                          e.activation(out=W[:, r0 * 128:(r0 + rn) * 128],
                                       in_=PSX[:, 0:rn * 128], func=AF.Copy),
                          deps=[t_t] + war[sl2])
                pstok[0] = ev
                evs.append(ev)
            qnb_tok[qn] = evs
            t_d = p.dma("pool", lambda e, W=WSL[sl2], dstQW=dstQW, ft=ft:
                        e.dma_start(out=dstQW[ft * P:(ft + 1) * P, :],
                                    in_=W[:, 0:D]),
                        deps=evs, key="dp%s%d" % (name, sl2))
            dump_tok[(name, ft, 0)] = t_d
            del war[sl2][:]
            war[sl2].append(t_d)
            evs_direct[(name, ft)] = evs

    def feed(ft, t_m1, t_m2):
        war_w1[ft % 2].append(t_m1)
        war_w2[ft % 2].append(t_m2)

    NC3 = W3CG // P
    QN_W3 = [HFM[:, j * W3CG:(j + 1) * W3CG] for j in range(2)]
    TT_W3 = [HFM[:, (2 + j) * W3CG:(3 + j) * W3CG] for j in range(2)]
    assert 4 * W3CG <= DC * TT
    w3qn_tok = [None, None]
    w3tt_tok = [None, None]
    w3_ctr = [0]
    hfm_gate_box = [None]

    def emit_w3_step(rc, cg):
        i2 = w3_ctr[0] % 2
        qeng = "dve"
        w3_ctr[0] += 1
        sl = ws_ctr[0] % 3
        ws_ctr[0] += 1
        t_l = p.dma("sync", lambda e, sl=sl, rc=rc, cg=cg:
                    e.dma_start(out=WS[sl][:, 0:W3CG],
                                in_=w3_in[rc * P:(rc + 1) * P,
                                          cg * W3CG:(cg + 1) * W3CG]),
                    deps=[t_thr, ws_tok[sl]], key="ws%d" % sl)
        p.op(qeng, lambda e, i2=i2, sl=sl: e.tensor_scalar(
            out=QN_W3[i2][:, 0:W3CG], in0=WS[sl][:, 0:W3CG],
            scalar1=thr[:, 2:3], scalar2=None, op0=ALU.is_ge),
            deps=[t_l, w3qn_tok[i2], hfm_gate_box[0]])
        t_q = p.op(qeng, lambda e, i2=i2, sl=sl: e.scalar_tensor_tensor(
            out=QN_W3[i2][:, 0:W3CG], in0=WS[sl][:, 0:W3CG],
            scalar=thr[:, 6:7], in1=QN_W3[i2][:, 0:W3CG],
            op0=ALU.is_le, op1=ALU.subtract))
        ws_tok[sl] = t_q
        evs = []
        for ri, r0 in enumerate(range(0, NC3, 8)):
            rn = min(8, NC3 - r0)
            PSX = PSW if ri % 2 == 0 else PSB
            pstok = psw_state if ri % 2 == 0 else psbox
            t_t = p.te(lambda e, i2=i2, r0=r0, rn=rn, PSX=PSX: [
                e.transpose(PSX[:, k * 128:(k + 1) * 128],
                            QN_W3[i2][:, (r0 + k) * 128:(r0 + k + 1) * 128],
                            idn[:]) for k in range(rn)][-1],
                deps=[t_q, pstok[0]])
            ev = p.op("act", lambda e, i2=i2, r0=r0, rn=rn, PSX=PSX:
                      e.activation(
                          out=TT_W3[i2][:, r0 * 128:(r0 + rn) * 128],
                          in_=PSX[:, 0:rn * 128], func=AF.Copy),
                      deps=[t_t, w3tt_tok[i2], hfm_gate_box[0]])
            pstok[0] = ev
            evs.append(ev)
        w3qn_tok[i2] = evs
        t_d = p.dma("pool", lambda e, i2=i2, rc=rc, cg=cg:
                    e.dma_start(out=QW3[rc * P:(rc + 1) * P,
                                       cg * W3CG:(cg + 1) * W3CG],
                                in_=TT_W3[i2][:, 0:W3CG]),
                    deps=evs, key="dw3%d" % i2)
        w3tt_tok[i2] = t_d
        dump_tok[("w3", rc, cg)] = t_d

    def emit_w3_row(r):
        for cg in range(NW3G):
            emit_w3_step(r, cg)

    phase_f1 = p.join() + p.dma_join()
    w3dep_box = [None]
    emit_pair_lq(0)
    emit_pair_lq(1)
    emit_pair_tx(0)
    direct_cfg = dict(
        evs=evs_direct, war1=war_w1, war2=war_w2, feed=feed,
        w3war=lambda: [(p.dsem[k], p.dcnt[k]) for k in ("dw30", "dw31")
                       if k in p.dsem])
    w3_next = [0]
    for marker in emit_ffn(False, phase_f1, w3dep=w3dep_box, direct=direct_cfg,
                           psbox=psbox):
        kind, tt, idx = marker
        if tt == 0 and kind == "ft":
            if idx + 2 < FC:
                emit_pair_lq(idx + 2)
            if idx + 1 < FC:
                emit_pair_tx(idx + 1)
        elif tt == 0 and kind == "dt":
            if w3dep_box[0] is None:
                w3dep_box[0] = (p.join()
                                + [(p.dsem[k2], p.dcnt[k2]) for k2 in p.dsem
                                   if k2.startswith("dp")])
                hfm_gate_box[0] = (p.sem["te"], p.cnt["te"])
                for r in range(min(3, DC)):
                    emit_w3_row(r)
                w3_next[0] = 3
            if idx + 2 >= w3_next[0] and w3_next[0] < DC:
                emit_w3_row(w3_next[0])
                w3_next[0] += 1
    if debug:
        p.dma("pool", lambda e: e.dma_start(out=dbg1[:, :], in_=R[:, :]),
              deps=r_tok + p.join(), key="dbg")

    ATTN_SENTINEL = True
    AK = ARB[:, o_ak:o_ak + KC * S]
    AVR = ARB[:, o_av:o_av + T128 * KVD]
    AV3 = AVR.rearrange("p (b v) -> p b v", b=T128)
    SLA = ARB[:, o_sa:o_sa + DC * TT]
    SLA3 = SLA.rearrange("p (c t) -> p c t", c=DC)
    AQ = ARB[:, o_sa:o_sa + NH * TT]
    SLB = ARB[:, o_sb:o_sb + sz_sb]
    SLB_T = ARB[:, o_sb:o_sb + T128 * TT]
    SLB3 = SLB_T.rearrange("p (b t) -> p b t", b=T128)
    SLC = ARB[:, o_sc:o_sc + sz_sc]
    SLC3 = ARB[:, o_sc:o_sc + NH * TT].rearrange("p (c t) -> p c t", c=NH)
    ABUF = ([ARB[:, o_ab + i * S:o_ab + (i + 1) * S] for i in range(2)]
            + [ARB[:, o_hq + D:o_hq + D + S]])
    assert o_hq + D + S <= NBF
    KSTG = [ARB[:, o_ks + i * TT:o_ks + (i + 1) * TT] for i in range(2)]
    HQSTG = ARB[:, o_hq:o_hq + D]

    def emit_attention(phase_dep):
        t_wkl = [p.dma("sync", lambda e, kc=kc: e.dma_start(
            out=SLB[:, kc * D:(kc + 1) * D], in_=AWK[kc * P:(kc + 1) * P, :]),
            deps=[t_dumps, phase_dep], key="awk") for kc in range(KC)]
        t_wkl = [t_wkl[-1]] * KC
        t_wvl = [p.dma("sync", lambda e, kc=kc: e.dma_start(
            out=SLC[:, kc * D:(kc + 1) * D], in_=AWV[kc * P:(kc + 1) * P, :]),
            deps=[t_dumps, phase_dep], key="awv") for kc in range(KC)]
        t_wvl = [t_wvl[-1]] * KC

        psb_tok = phase_dep
        psg0_tok = phase_dep
        psg1_tok = phase_dep
        psf_a_tok = phase_dep
        kstg_tok = [None, None]
        sla_te = None
        stg_te = None
        t_k_ready = [[None] * KC for _ in range(NTT)]
        t_v_ready = [[None] * KC for _ in range(NTT)]
        for tt in range(NTT):
            t_hfm_ev = []
            for u in range(TSUB):
                t = tt * TSUB + u
                t_n = emit_ln(t * D, HQSTG[:, 0:D],
                              [r_tok[t], phase_dep, sla_te, stg_te])
                for k0 in range(0, DC, 8):
                    kn = min(8, DC - k0)
                    t_t = p.te(lambda e, k0=k0, kn=kn: [
                        e.transpose(PSB[:, k * 128:(k + 1) * 128],
                                    HQSTG[:, (k0 + k) * 128:(k0 + k + 1) * 128],
                                    idn[:]) for k in range(kn)][-1],
                        deps=[t_n, psb_tok, t_consts])
                    stg_te = t_t
                    t_e = p.op("act", lambda e, k0=k0, kn=kn, u=u: e.activation(
                        out=SLA3[:, k0:k0 + kn, u * 128:(u + 1) * 128],
                        in_=PSB[:, 0:kn * 128], func=AF.Copy), deps=[t_t])
                    psb_tok = t_e
                    t_hfm_ev.append(t_e)
            for kc in range(KC):
                sl = kc % 2
                kq = sl * 512
                t_mk = p.te(lambda e, kc=kc, kq=kq: [
                    e.matmul(PSG[:, kq:kq + TT],
                             SLB[:, kc * D + k * 128:kc * D + (k + 1) * 128],
                             SLA[:, k * TT:(k + 1) * TT],
                             start=(k == 0), stop=(k == DC - 1))
                    for k in range(DC)][-1],
                    deps=t_hfm_ev + [t_wkl[kc],
                                     psg0_tok if sl == 0 else psg1_tok])
                sla_te = t_mk
                t_ek = p.op("act", lambda e, sl=sl, kq=kq: e.activation(
                    out=KSTG[sl][:, :], in_=PSG[:, kq:kq + TT], func=AF.Copy),
                    deps=[t_mk, kstg_tok[sl]])
                if sl == 0:
                    psg0_tok = t_ek
                else:
                    psg1_tok = t_ek
                t_pm = p.te(lambda e, sl=sl: e.matmul(
                    PSF[:, 0:TT], pmt[:], KSTG[sl][:, :],
                    start=True, stop=True), deps=[t_ek, psf_a_tok])
                p.op("dve", lambda e, sl=sl, tt=tt: e.tensor_tensor(
                    out=SC1[:, :], in0=KSTG[sl][:, :],
                    in1=ropec[:, tt * TT:(tt + 1) * TT], op=ALU.mult),
                    deps=[t_ek])
                t_t2 = p.op("dve", lambda e, tt=tt: e.tensor_tensor(
                    out=SC2[:, :], in0=PSF[:, 0:TT],
                    in1=ropes[:, tt * TT:(tt + 1) * TT], op=ALU.mult),
                    deps=[t_pm])
                psf_a_tok = t_t2
                t_kf = p.op("dve", lambda e, kc=kc, tt=tt: e.tensor_tensor(
                    out=AK[:, kc * S + tt * TT:kc * S + (tt + 1) * TT],
                    in0=SC1[:, :], in1=SC2[:, :], op=ALU.add))
                kstg_tok[sl] = t_kf
                t_k_ready[tt][kc] = t_kf
            for vc in range(KC):
                sl = vc % 2
                kq = sl * 512
                t_mv = p.te(lambda e, vc=vc, kq=kq: [
                    e.matmul(PSG[:, kq:kq + TT],
                             SLC[:, vc * D + k * 128:vc * D + (k + 1) * 128],
                             SLA[:, k * TT:(k + 1) * TT],
                             start=(k == 0), stop=(k == DC - 1))
                    for k in range(DC)][-1],
                    deps=t_hfm_ev + [t_wvl[vc],
                                     psg0_tok if sl == 0 else psg1_tok])
                sla_te = t_mv
                t_ev = p.op("act", lambda e, sl=sl, kq=kq: e.activation(
                    out=KSTG[sl][:, :], in_=PSG[:, kq:kq + TT], func=AF.Copy),
                    deps=[t_mv, kstg_tok[sl]])
                if sl == 0:
                    psg0_tok = t_ev
                else:
                    psg1_tok = t_ev
                t_tv = p.te(lambda e, sl=sl: [
                    e.transpose(PSB[:, u * 128:(u + 1) * 128],
                                KSTG[sl][:, u * 128:(u + 1) * 128],
                                idn[:]) for u in range(TSUB)][-1],
                    deps=[t_ev, psb_tok])
                kstg_tok[sl] = t_tv
                t_evv = p.op("act", lambda e, vc=vc, tt=tt: e.activation(
                    out=AV3[:, tt * TSUB:tt * TSUB + TSUB, vc * 128:(vc + 1) * 128],
                    in_=PSB[:, 0:TSUB * 128], func=AF.Copy), deps=[t_tv])
                psb_tok = t_evv
                t_v_ready[tt][vc] = t_evv

        t_kv_join = p.join()
        abuf_tok = [t_kv_join, t_kv_join, t_kv_join]
        psfq4 = [t_kv_join] * 4
        ssb_tok = [t_kv_join, t_kv_join, t_kv_join]
        stats_tok = [t_kv_join, t_kv_join, t_kv_join]
        rs_tok = [t_kv_join, t_kv_join, t_kv_join]
        psx_rr = [0]
        pending_wo = [None]
        psx_tok = [psb_tok, t_kv_join]
        psgh = [psg0_tok, psg1_tok]
        kst2_tok = [t_kv_join, t_kv_join]
        o3_war = t_kv_join
        slb_te = t_kv_join
        stg_te = None
        for qt in range(NTT):
            t_hq_ev = []
            for u in range(TSUB):
                t = qt * TSUB + u
                t_n = emit_ln(t * D, HQSTG[:, 0:D], [r_tok[t], slb_te, stg_te])
                for k0 in range(0, DC, 8):
                    kn = min(8, DC - k0)
                    t_t = p.te(lambda e, k0=k0, kn=kn: [
                        e.transpose(PSB[:, k * 128:(k + 1) * 128],
                                    HQSTG[:, (k0 + k) * 128:(k0 + k + 1) * 128],
                                    idn[:]) for k in range(kn)][-1],
                        deps=[t_n, psx_tok[0]])
                    stg_te = t_t
                    t_e = p.op("act", lambda e, k0=k0, kn=kn, u=u: e.activation(
                        out=SLB3[:, k0:k0 + kn, u * 128:(u + 1) * 128],
                        in_=PSB[:, 0:kn * 128], func=AF.Copy), deps=[t_t])
                    psx_tok[0] = t_e
                    t_hq_ev.append(t_e)
            t_aq = [None] * NH
            for oc in range(NH):
                sl = oc % 2
                t_lq = p.dma("pool", lambda e, sl=sl, oc=oc: e.dma_start(
                    out=ABUF[sl][:, 0:D], in_=AWQ[oc * P:(oc + 1) * P, :]),
                    deps=[t_dumps, abuf_tok[sl]], key="ab%d" % sl)
                t_mq = p.te(lambda e, sl=sl, q=(oc % 2) * 512: [
                    e.matmul(PSG[:, q:q + TT],
                             ABUF[sl][:, k * 128:(k + 1) * 128],
                             SLB[:, k * TT:(k + 1) * TT],
                             start=(k == 0), stop=(k == DC - 1))
                    for k in range(DC)][-1],
                    deps=t_hq_ev + [t_lq, psgh[oc % 2]])
                abuf_tok[sl] = t_mq
                t_eq = p.op("act", lambda e, sl=sl, q=(oc % 2) * 512: e.activation(
                    out=KSTG[sl][:, :], in_=PSG[:, q:q + TT], func=AF.Copy,
                    scale=-INVSQ), deps=[t_mq, kst2_tok[sl]])
                psgh[oc % 2] = t_eq
                t_pm = p.te(lambda e, sl=sl, q=(oc % 2) * 1024: e.matmul(
                    PSF[:, q:q + TT], pmt[:], KSTG[sl][:, :],
                    start=True, stop=True), deps=[t_eq, psfq4[(oc % 2) * 2]])
                p.op("dve", lambda e, sl=sl, qt=qt: e.tensor_tensor(
                    out=SC1[:, :], in0=KSTG[sl][:, :],
                    in1=ropec[:, qt * TT:(qt + 1) * TT], op=ALU.mult),
                    deps=[t_eq])
                t_t2 = p.op("dve", lambda e, q=(oc % 2) * 1024, qt=qt:
                            e.tensor_tensor(
                    out=SC2[:, :], in0=PSF[:, q:q + TT],
                    in1=ropes[:, qt * TT:(qt + 1) * TT], op=ALU.mult),
                    deps=[t_pm])
                psfq4[(oc % 2) * 2] = t_t2
                t_q = p.op("dve", lambda e, oc=oc: e.tensor_tensor(
                    out=AQ[:, oc * TT:(oc + 1) * TT],
                    in0=SC1[:, :], in1=SC2[:, :], op=ALU.add))
                kst2_tok[sl] = t_q
                t_aq[oc] = t_q
            pairs = [(h, u) for h in range(NH) for u in range(TSUB)]
            stA = {}
            at_ev = {h: [] for h in range(NH)}
            t_o_ev = None

            def stage_a(idx):
                h, u = pairs[idx]
                g = h // GROUPS
                j = qt * TSUB + u
                klen = (j + 1) * 128
                nkc = (klen + 511) // 512
                par = idx % 3
                t_ev = None
                for kc in range(nkc):
                    n = min(512, klen - kc * 512)
                    diag = (kc == nkc - 1)

                    def mk_sc(e, kc=kc, n=n, diag=diag, h=h, g=g, u=u, klen=klen):
                        i1 = e.matmul(
                            PSF[:, kc * 512:kc * 512 + n],
                            AQ[:, h * TT + u * 128:h * TT + (u + 1) * 128],
                            AK[:, g * S + kc * 512:g * S + kc * 512 + n],
                            start=True, stop=not diag)
                        if diag:
                            i1 = e.matmul(
                                PSF[:, klen - 128:klen],
                                idn[:], tmk[:], start=False, stop=True)
                        return i1
                    t_scc = p.te(mk_sc, deps=[t_aq[h], t_k_ready[qt][g],
                                              t_consts, psfq4[kc]])
                    ev_deps = [t_scc]
                    if kc == 0:
                        ev_deps.append(ssb_tok[par])
                    t_ev = p.op("dve", lambda e, kc=kc, n=n, par=par:
                                e.tensor_scalar(
                                    out=SSB[par][:, kc * 512:kc * 512 + n],
                                    in0=PSF[:, kc * 512:kc * 512 + n],
                                    scalar1=1.0,
                                    scalar2=(None if kc == 0
                                             else mxn2[:, par:par + 1]),
                                    op0=ALU.mult, op1=ALU.min,
                                    accum_out=mxn2[:, par:par + 1]),
                                deps=ev_deps)
                    psfq4[kc] = t_ev
                stA[idx] = (t_ev, klen, par, j)

            def stage_b(idx):
                t_ng, klen, par, j = stA[idx]
                t_ex = p.op("act", lambda e, klen=klen, par=par: e.activation(
                    out=ABUF[par][:, 0:klen], in_=SSB[par][:, 0:klen],
                    func=AF.Exp, scale=-1.0, bias=mxn2[:, par:par + 1],
                    accum_out=sums2[:, par:par + 1]),
                    deps=[t_ng, abuf_tok[par], stats_tok[par]])
                ssb_tok[par] = t_ex
                stA[idx] = (t_ex, klen, par, j)

            def stage_c(idx):
                nonlocal t_o_ev
                h, u = pairs[idx]
                t_ex, klen, par, j = stA[idx]
                del stA[idx]
                t_rc = p.op("dve", lambda e, par=par: e.reciprocal(
                    out=rs2[:, par:par + 1], in_=sums2[:, par:par + 1]),
                    deps=[t_ex, rs_tok[par]])
                stats_tok[par] = t_rc
                t_as = t_ex
                for kb0 in range(0, j + 1, 8):
                    rn = min(8, j + 1 - kb0)
                    t_as = p.op("pool", lambda e, kb0=kb0, rn=rn, par=par:
                                e.tensor_scalar(
                                    out=ABUF[par][:, kb0 * 128:(kb0 + rn) * 128],
                                    in0=ABUF[par][:, kb0 * 128:(kb0 + rn) * 128],
                                    scalar1=rs2[:, par:par + 1], scalar2=1.0,
                                    op0=ALU.mult, op1=ALU.mult),
                                deps=[t_rc, t_ex])
                    rs_tok[par] = t_as
                    bk = psx_rr[0] % 2
                    psx_rr[0] += 1
                    PSX = PSB if bk == 0 else PSW
                    t_tr = p.te(lambda e, par=par, kb0=kb0, rn=rn, PSX=PSX: [
                        e.transpose(PSX[:, k * 128:(k + 1) * 128],
                                    ABUF[par][:, (kb0 + k) * 128:(kb0 + k + 1) * 128],
                                    idn[:]) for k in range(rn)][-1],
                        deps=[t_as, psx_tok[bk]])
                    abuf_tok[par] = t_tr
                    t_ae = p.op("dve", lambda e, kb0=kb0, rn=rn, u=u, PSX=PSX:
                                e.tensor_copy(
                                    out=SLB3[:, kb0:kb0 + rn,
                                             u * 128:(u + 1) * 128],
                                    in_=PSX[:, 0:rn * 128]),
                                deps=[t_tr])
                    psx_tok[bk] = t_ae
                    at_ev[h].append(t_ae)
                if u == TSUB - 1:
                    emit_av(h)

            def emit_av(h):
                nonlocal t_o_ev
                g = h // GROUPS
                avq = (h % 2) * 512

                def mk_av(e, qt=qt, g=g, avq=avq):
                    insts = []
                    first = True
                    for kb in range(qt * TSUB):
                        insts.append(e.matmul(
                            PSG[:, avq:avq + TT],
                            AV3[:, kb, g * 128:(g + 1) * 128],
                            SLB3[:, kb, 0:TT],
                            start=first, stop=False))
                        first = False
                    for v in range(TSUB):
                        kb = qt * TSUB + v
                        insts.append(e.matmul(
                            PSG[:, avq + v * 128:avq + TT],
                            AV3[:, kb, g * 128:(g + 1) * 128],
                            SLB3[:, kb, v * 128:TT],
                            start=first, stop=(v == TSUB - 1)))
                        first = False
                    return insts[-1]
                t_av = p.te(mk_av, deps=at_ev[h]
                            + [t_v_ready[qt][g], psgh[h % 2]])
                t_oe = p.op("act", lambda e, h=h, avq=avq: e.activation(
                    out=SLC3[:, h, 0:TT], in_=PSG[:, avq:avq + TT],
                    func=AF.Copy), deps=[t_av, o3_war])
                psgh[h % 2] = t_oe
                t_o_ev = t_oe
                return t_av

            npair = len(pairs)
            for i in range(npair):
                stage_a(i)
                if i >= 1:
                    stage_b(i - 1)
                if i >= 2:
                    stage_c(i - 2)
            stage_b(npair - 1)
            stage_c(npair - 2)
            stage_c(npair - 1)
            slb_te = None

            def mk_wo(qt=qt, t_o_all=t_o_ev):
                o3_last = o3_war
                for oc in range(DC):
                    sl = oc % 2
                    woq = (oc % 2) * 512
                    t_lo = p.dma("pool", lambda e, sl=sl, oc=oc: e.dma_start(
                        out=ABUF[sl][:, 0:D], in_=AWO[oc * P:(oc + 1) * P, :]),
                        deps=[t_dumps, abuf_tok[sl]], key="ab%d" % sl)
                    t_mo = p.te(lambda e, sl=sl, woq=woq: [
                        e.matmul(PSG[:, woq:woq + TT],
                                 ABUF[sl][:, hc * 128:(hc + 1) * 128],
                                 SLC3[:, hc, 0:TT],
                                 start=(hc == 0), stop=(hc == NH - 1))
                        for hc in range(NH)][-1],
                        deps=[t_lo, t_o_all, psgh[oc % 2]])
                    abuf_tok[sl] = t_mo
                    o3_last = t_mo
                    t_eo = p.op("act", lambda e, sl=sl, woq=woq: e.activation(
                        out=KSTG[sl][:, :], in_=PSG[:, woq:woq + TT],
                        func=AF.Copy), deps=[t_mo, kst2_tok[sl]])
                    psgh[oc % 2] = t_eo
                    PSX = PSB if oc % 2 == 0 else PSW
                    t_to = p.te(lambda e, sl=sl, PSX=PSX: [
                        e.transpose(PSX[:, u * 128:(u + 1) * 128],
                                    KSTG[sl][:, u * 128:(u + 1) * 128],
                                    idn[:]) for u in range(TSUB)][-1],
                        deps=[t_eo, psx_tok[oc % 2]])
                    kst2_tok[sl] = t_to
                    t_r = p.op("dve", lambda e, qt=qt, oc=oc, PSX=PSX:
                               e.scalar_tensor_tensor(
                                   out=R3[:, qt * TSUB:(qt + 1) * TSUB,
                                          oc * 128:(oc + 1) * 128],
                                   in0=PSX[:, 0:TSUB * 128], scalar=1.0,
                                   in1=R3[:, qt * TSUB:(qt + 1) * TSUB,
                                          oc * 128:(oc + 1) * 128],
                                   op0=ALU.mult, op1=ALU.add), deps=[t_to])
                    for u in range(TSUB):
                        r_tok[qt * TSUB + u] = t_r
                    psx_tok[oc % 2] = t_r
                return o3_last
            o3_war = mk_wo()

    phase_b = p.join() + p.dma_join()
    emit_attention(phase_b)
    if debug:
        p.dma("pool", lambda e: e.dma_start(out=dbg2[:, :], in_=R[:, :]),
              deps=r_tok + p.join(), key="dbg")


    phase_f2 = p.join() + p.dma_join()
    ob = [None]
    for _ in emit_ffn(True, phase_f2, out_box=ob):
        pass
    last_out = ob[0]

    p.barrier_wait("pool", p.dma_join())
    p.barrier_wait("sync", p.dma_join())

    p.finalize()
    return nc


def make_consts(cfg):
    S, D, HD, theta = cfg["S"], cfg["D"], cfg["HD"], cfg["theta"]
    cid = np.eye(P, dtype=np.float32).astype(BF)
    r = np.arange(P)
    ctm = np.where(r[None, :] > r[:, None], np.float32(NEG),
                   np.float32(0.0)).astype(BF)
    cpm = np.zeros((P, P), dtype=np.float32)
    cpm[r, r ^ 1] = 1.0
    cpm = cpm.astype(BF)
    freqs = (1.0 / (np.float32(theta) ** (np.arange(0, HD, 2, dtype=np.float32) / np.float32(HD)))).astype(np.float32)
    s = np.arange(S, dtype=np.float32)
    ang = s[None, :] * freqs[:, None]
    cos = np.cos(ang).astype(np.float32)
    sin = np.sin(ang).astype(np.float32)
    ropec = np.repeat(cos, 2, axis=0)
    sgn = np.where(np.arange(HD) % 2 == 0, -1.0, 1.0).astype(np.float32)
    ropes = np.repeat(sin, 2, axis=0) * sgn[:, None]
    cmisc = np.zeros((P, 4), dtype=np.float32)
    cmisc[:, 0] = 1e-5
    cmisc[:, 1] = 1.0
    conesr = np.ones((1, P), dtype=np.float32)
    return dict(cid=cid, ctm=ctm, cpm=cpm,
                cropec=np.ascontiguousarray(ropec.astype(BF)),
                cropes=np.ascontiguousarray(ropes.astype(BF)),
                cmisc=cmisc, conesr=conesr)


def make_in_maps(cfg, x_full, Wq, Wk, Wv, Wo, w1, w2, w3):
    ncores = cfg["ncores"]
    SHW = cfg["F"] * cfg["D"] // P
    consts = make_consts(cfg)
    f32 = lambda a: np.ascontiguousarray(np.asarray(a, dtype=np.float32))
    Wq, Wk, Wv, Wo = f32(Wq), f32(Wk), f32(Wv), f32(Wo)
    w1, w2, w3 = f32(w1), f32(w2), f32(w3)
    w1f, w2f, w3f = w1.reshape(-1), w2.reshape(-1), w3.reshape(-1)
    in_maps = []
    for c in range(ncores):
        m = dict(consts)
        m["x"] = f32(x_full[c])
        m["wq"], m["wk"], m["wv"], m["wo"] = Wq, Wk, Wv, Wo
        m["w1"], m["w2"], m["w3"] = w1, w2, w3
        m["gsh1"] = w1f.reshape(P, SHW)
        m["gsh2"] = w2f.reshape(P, SHW)
        m["gsh3"] = w3f.reshape(P, SHW)
        in_maps.append(m)
    return in_maps


_BUILT = {}


def kernel(L0_attn_x, L0_attn_Wq, L0_attn_Wk, L0_attn_Wv, L0_attn_Wo,
           L0_ffn_w1, L0_ffn_w2, L0_ffn_w3):
    cfg = full_cfg()
    key = "full"
    if key not in _BUILT:
        _BUILT[key] = build_program(cfg, debug=False)
    nc = _BUILT[key]
    in_maps = make_in_maps(cfg, np.asarray(L0_attn_x), L0_attn_Wq, L0_attn_Wk,
                           L0_attn_Wv, L0_attn_Wo, L0_ffn_w1, L0_ffn_w2,
                           L0_ffn_w3)
    res = run_bass_kernel_spmd(nc, in_maps, core_ids=list(range(cfg["ncores"])))
    out = np.stack([np.asarray(res.results[i]["y"], dtype=np.float32)
                    for i in range(cfg["ncores"])], axis=0)
    return out


# version 19
# speedup vs baseline: 1.0809x; 1.0809x over previous
import math
import numpy as np
import ml_dtypes

import concourse.bass as bass
import concourse.mybir as mybir
from concourse.bass_utils import run_bass_kernel_spmd

F32 = mybir.dt.float32
BF16 = mybir.dt.bfloat16
AF = mybir.ActivationFunctionType
ALU = mybir.AluOpType
AX = mybir.AxisListType
BF = ml_dtypes.bfloat16

P = 128
NEG = 1.0e30


def full_cfg():
    return dict(S=2048, D=2048, F=5632, HD=128, NKV=4, theta=1e4,
                TT=512, ncores=8)


def small_cfg():
    return dict(S=1024, D=512, F=1024, HD=128, NKV=1, theta=1e4,
                TT=512, ncores=2)


class Prog:
    ENGINES = ("te", "act", "dve", "pool", "sync")
    QUEUES = ("sync", "pool")

    def __init__(self, nc):
        self.nc = nc
        self.sem = {k: nc.alloc_semaphore("sem_" + k) for k in self.ENGINES}
        self.dsem = {}
        self.cnt = {k: 0 for k in self.ENGINES}
        self.dcnt = {}
        self.ops = {k: [] for k in self.ENGINES}

    def _norm(self, deps):
        out = []
        for d in deps:
            if d is None:
                continue
            if isinstance(d, list):
                out.extend(t for t in d if t is not None)
            else:
                out.append(d)
        dd = {}
        for (s, v) in out:
            if v <= 0:
                continue
            k = id(s)
            if k not in dd or dd[k][1] < v:
                dd[k] = (s, v)
        return list(dd.values())

    def _emit(self, key, fn, deps, inc_sem, inc_n):
        waits = self._norm(deps)

        def run(eng):
            for (s, v) in waits:
                eng.wait_ge(s, v)
            inst = fn(eng)
            if inc_sem is not None:
                inst.then_inc(inc_sem, inc_n)
        self.ops[key].append(run)

    def op(self, key, fn, deps=()):
        deps = list(deps)
        if key in ("act", "dve", "pool") and self.cnt[key] > 0:
            deps.append((self.sem[key], self.cnt[key]))
        self._emit(key, fn, deps, self.sem[key], 1)
        self.cnt[key] += 1
        return (self.sem[key], self.cnt[key])

    def te(self, fn, deps=()):
        self._emit("te", fn, deps, self.sem["te"], 1)
        self.cnt["te"] += 1
        return (self.sem["te"], self.cnt["te"])

    def dma(self, queue, fn, deps=(), key=None):
        key = key or queue
        if key not in self.dsem:
            self.dsem[key] = self.nc.alloc_semaphore("dsem_" + key)
            self.dcnt[key] = 0
        self._emit(queue, fn, deps, self.dsem[key], 16)
        self.dcnt[key] += 16
        return (self.dsem[key], self.dcnt[key])

    def dma_join(self):
        return [(self.dsem[k], self.dcnt[k]) for k in self.dsem]

    def barrier_wait(self, key, deps):
        self._emit(key, lambda e: None, deps, None, 0)

    def join(self):
        return [(self.sem[k], self.cnt[k]) for k in ("te", "act", "dve")]

    def finalize(self):
        nc = self.nc
        with nc.Block() as blk:
            @blk.tensor
            def _(e):
                for f in self.ops["te"]:
                    f(e)

            @blk.scalar
            def _(e):
                for f in self.ops["act"]:
                    f(e)

            @blk.vector
            def _(e):
                for f in self.ops["dve"]:
                    f(e)

            @blk.gpsimd
            def _(e):
                for f in self.ops["pool"]:
                    f(e)

            @blk.sync
            def _(e):
                for f in self.ops["sync"]:
                    f(e)


def build_program(cfg, debug=False):
    S, D, F, HD = cfg["S"], cfg["D"], cfg["F"], cfg["HD"]
    NKV, TT = cfg["NKV"], cfg["TT"]
    ncores = cfg["ncores"]
    NH = D // HD
    KVD = NKV * HD
    GROUPS = NH // NKV
    DC = D // P
    FC = F // P
    KC = KVD // P
    T128 = S // P
    NTT = S // TT
    TSUB = TT // P
    NW = F * D
    SHW = NW // P
    INVSQ = 1.0 / math.sqrt(HD)
    W3CG = None
    for c in range(2048, 0, -128):
        if F % c == 0:
            W3CG = c
            break
    NW3G = F // W3CG
    PCG = max(min(D, 2048), W3CG)

    nc = bass.Bass("TRN2", target_bir_lowering=False, debug=False)

    x_in = nc.dram_tensor("x", [S, D], F32, kind="ExternalInput").ap()
    wq_in = nc.dram_tensor("wq", [D, D], F32, kind="ExternalInput").ap()
    wk_in = nc.dram_tensor("wk", [KVD, D], F32, kind="ExternalInput").ap()
    wv_in = nc.dram_tensor("wv", [KVD, D], F32, kind="ExternalInput").ap()
    wo_in = nc.dram_tensor("wo", [D, D], F32, kind="ExternalInput").ap()
    w1_in = nc.dram_tensor("w1", [F, D], F32, kind="ExternalInput").ap()
    w2_in = nc.dram_tensor("w2", [F, D], F32, kind="ExternalInput").ap()
    w3_in = nc.dram_tensor("w3", [D, F], F32, kind="ExternalInput").ap()
    gsh1 = nc.dram_tensor("gsh1", [P, SHW], F32, kind="ExternalInput").ap()
    gsh2 = nc.dram_tensor("gsh2", [P, SHW], F32, kind="ExternalInput").ap()
    gsh3 = nc.dram_tensor("gsh3", [P, SHW], F32, kind="ExternalInput").ap()
    cid = nc.dram_tensor("cid", [P, P], BF16, kind="ExternalInput").ap()
    ctm = nc.dram_tensor("ctm", [P, P], BF16, kind="ExternalInput").ap()
    cpm = nc.dram_tensor("cpm", [P, P], BF16, kind="ExternalInput").ap()
    cropec = nc.dram_tensor("cropec", [P, S], BF16, kind="ExternalInput").ap()
    cropes = nc.dram_tensor("cropes", [P, S], BF16, kind="ExternalInput").ap()
    cmisc = nc.dram_tensor("cmisc", [P, 4], F32, kind="ExternalInput").ap()
    conesr = nc.dram_tensor("conesr", [1, P], F32, kind="ExternalInput").ap()
    y_out = nc.dram_tensor("y", [S, D], F32, kind="ExternalOutput").ap()
    if debug:
        dbg1 = nc.dram_tensor("dbg1", [P, T128 * D], BF16, kind="ExternalOutput").ap()
        dbg2 = nc.dram_tensor("dbg2", [P, T128 * D], BF16, kind="ExternalOutput").ap()
        dbgt = nc.dram_tensor("dbgt", [P, 8], F32, kind="ExternalOutput").ap()

    QW1 = nc.dram_tensor("QW1", [F, D], BF16).ap()
    QW2 = nc.dram_tensor("QW2", [F, D], BF16).ap()
    QW3 = nc.dram_tensor("QW3", [D, F], BF16).ap()
    AWQ = nc.dram_tensor("AWQ", [D, D], BF16).ap()
    AWK = nc.dram_tensor("AWK", [KVD, D], BF16).ap()
    AWV = nc.dram_tensor("AWV", [KVD, D], BF16).ap()
    AWO = nc.dram_tensor("AWO", [D, D], BF16).ap()

    ffn_sz = DC * TT + FC * TT + 2048 + 2 * D + 2 * D + 2 * F
    o_ak = 0
    o_av = o_ak + KC * S
    o_sa = o_av + T128 * KVD
    o_sb = o_sa + DC * TT
    sz_sb = max(T128 * TT, DC * TT, KC * D)
    o_sc = o_sb + sz_sb
    sz_sc = max(DC * TT, KC * D)
    o_ab = o_sc + sz_sc
    o_ks = o_ab + 2 * S
    o_hq = o_ks + 2 * TT
    attn_sz = o_hq + D
    prep_sz = 7 * PCG + 2048
    NBF = max(ffn_sz, attn_sz, prep_sz)

    R = nc.alloc_sbuf_tensor("R", [P, T128 * D], BF16)
    ARB = nc.alloc_sbuf_tensor("ARB", [P, NBF], BF16)
    ARF = nc.alloc_sbuf_tensor("ARF", [P, 3 * 2048 + 2 * 512], F32)
    idn = nc.alloc_sbuf_tensor("idn", [P, P], BF16)
    tmk = nc.alloc_sbuf_tensor("tmk", [P, P], BF16)
    pmt = nc.alloc_sbuf_tensor("pmt", [P, P], BF16)
    ropec = nc.alloc_sbuf_tensor("ropec", [P, S], BF16)
    ropes = nc.alloc_sbuf_tensor("ropes", [P, S], BF16)
    misc = nc.alloc_sbuf_tensor("misc", [P, 4], F32)
    onesr = nc.alloc_sbuf_tensor("onesr", [1, P], F32)
    stats = nc.alloc_sbuf_tensor("stats", [P, 24], F32)
    mv = nc.alloc_sbuf_tensor("mv", [P, 2], F32)
    rstd = nc.alloc_sbuf_tensor("rstd", [P, 1], F32)
    ngch_max = 3 * ((SHW + 2047) // 2048)
    gpart = nc.alloc_sbuf_tensor("gpart", [P, max(24, ngch_max)], F32)
    g3 = nc.alloc_sbuf_tensor("g3", [P, 4], F32)
    gsb = nc.alloc_sbuf_tensor("gsb", [1, 4], F32)
    gsum2 = nc.alloc_sbuf_tensor("gsum2", [1, 4], F32)
    thr = nc.alloc_sbuf_tensor("thr", [P, 8], F32)
    mx = nc.alloc_sbuf_tensor("mx", [P, 1], F32)
    sums = nc.alloc_sbuf_tensor("sums", [P, 1], F32)
    rs = nc.alloc_sbuf_tensor("rs", [P, 1], F32)
    mxp2 = nc.alloc_sbuf_tensor("mxp2", [P, 2], F32)
    mxn2 = nc.alloc_sbuf_tensor("mxn2", [P, 4], F32)
    sums2 = nc.alloc_sbuf_tensor("sums2", [P, 4], F32)
    rs2 = nc.alloc_sbuf_tensor("rs2", [P, 4], F32)

    PSF = nc.alloc_psum_tensor("PSF", [P, 2048], F32)
    PSG = nc.alloc_psum_tensor("PSG", [P, 1024], F32)
    PSB = nc.alloc_psum_tensor("PSB", [P, 1024], BF16)
    PSW = nc.alloc_psum_tensor("PSW", [P, 1024], BF16)

    WS = [ARF[:, 0:2048], ARF[:, 2048:4096], ARF[:, 4096:6144]]
    SC1 = ARF[:, 6144:6656]
    SC2 = ARF[:, 6656:7168]
    OUTS = [SC1, SC2]
    SSB = [ARF[:, 0:2048], ARF[:, 2048:4096],
           ARF[:, 4096:6144]]
    R3 = R.rearrange("p (t d) -> p t d", t=T128)
    Y4 = y_out.rearrange("(b p) d -> p b d", p=P)

    eps_ap = misc[:, 0:1]
    ones1 = misc[:, 1:2]

    p = Prog(nc)

    tcs = None
    for dst, src in ((idn, cid), (tmk, ctm), (pmt, cpm), (ropec, cropec),
                     (ropes, cropes), (misc, cmisc), (onesr, conesr)):
        tcs = p.dma("sync", lambda e, d=dst, s=src: e.dma_start(out=d[:], in_=s[:, :]),
                    key="consts")
    t_consts = tcs

    gchunks = []
    off = 0
    while off < SHW:
        c = min(2048, SHW - off)
        gchunks.append((off, c))
        off += c
    ngch = len(gchunks)

    ws_tok = [None, None, None]
    ws_ctr = [0]
    gs_tok = [None, None]
    gs_ctr = [0]
    dump_junk = ARB[:, 7 * PCG:7 * PCG + 2048]
    qn_tok = [None, None, None]
    tt_tok = [None, None, None, None]
    tt_ctr = [0]
    psw_state = [None, None]
    pvb_ctr = [0]
    dump_tok = {}
    gp_state = {"idx": 0, "last": None}

    g_steps = []
    for m, gsh in enumerate((gsh1, gsh2, gsh3)):
        for (o, c) in gchunks:
            def step(m=m, gsh=gsh, o=o, c=c):
                sl = ws_ctr[0] % 3
                ws_ctr[0] += 1
                k = gp_state["idx"]
                gp_state["idx"] += 1
                t_l = p.dma("sync", lambda e, sl=sl, g=gsh, o=o, c=c:
                            e.dma_start(out=WS[sl][:, 0:c], in_=g[:, o:o + c]),
                            deps=[t_consts, ws_tok[sl]], key="ws%d" % sl)
                t_abs = p.op("act", lambda e, sl=sl, c=c, k=k:
                             e.activation(out=dump_junk[:, 0:c],
                                          in_=WS[sl][:, 0:c],
                                          func=AF.Abs,
                                          accum_out=gpart[:, k:k + 1]),
                             deps=[t_l])
                ws_tok[sl] = t_abs
                gp_state["last"] = t_abs
            g_steps.append(step)

    QNAT = [ARB[:, j * PCG:(j + 1) * PCG] for j in range(3)]
    TTILE = [ARB[:, (3 + i) * PCG:(4 + i) * PCG] for i in range(4)]

    def prep_steps(name, src_ap, dst_ap, rows, cols, CG, quant, thr_idx, dep0f,
                   QN=None, TTL=None, qn_t=None, tt_t=None, banks=(0, 1),
                   dkey="dump"):
        QN = QN if QN is not None else QNAT
        TTL = TTL if TTL is not None else TTILE
        qn_t = qn_t if qn_t is not None else qn_tok
        tt_t = tt_t if tt_t is not None else tt_tok
        nqn = len(QN)
        ntt = len(TTL)
        NRC = rows // P
        NCGr = cols // CG
        NCB = CG // P
        steps = []
        for rc in range(NRC):
            for cg in range(NCGr):
                def step(rc=rc, cg=cg):
                    i = ws_ctr[0] % nqn
                    it = tt_ctr[0] % ntt
                    tt_ctr[0] += 1
                    ws_ctr[0] += 1
                    t_l = p.dma("sync", lambda e, i=i, rc=rc, cg=cg:
                                e.dma_start(out=WS[i % 3][:, 0:CG],
                                            in_=src_ap[rc * P:(rc + 1) * P,
                                                       cg * CG:(cg + 1) * CG]),
                                deps=[dep0f(), ws_tok[i % 3]], key="ws%d" % (i % 3))
                    if quant:
                        p.op("dve", lambda e, i=i: e.tensor_scalar(
                            out=QN[i][:, 0:CG], in0=WS[i % 3][:, 0:CG],
                            scalar1=thr[:, thr_idx:thr_idx + 1], scalar2=None,
                            op0=ALU.is_ge), deps=[t_l, qn_t[i]])
                        t_q = p.op("dve", lambda e, i=i: e.scalar_tensor_tensor(
                            out=QN[i][:, 0:CG], in0=WS[i % 3][:, 0:CG],
                            scalar=thr[:, 4 + thr_idx:5 + thr_idx],
                            in1=QN[i][:, 0:CG], op0=ALU.is_le,
                            op1=ALU.subtract))
                    else:
                        t_q = p.op("dve", lambda e, i=i: e.tensor_copy(
                            out=QN[i][:, 0:CG], in_=WS[i % 3][:, 0:CG]),
                            deps=[t_l, qn_t[i]])
                    ws_tok[i % 3] = t_q
                    evs = []
                    first = True
                    for r0 in range(0, NCB, 8):
                        rn = min(8, NCB - r0)
                        bk = banks[pvb_ctr[0] % len(banks)]
                        pvb_ctr[0] += 1
                        PSX = PSW if bk == 0 else PSB
                        tdeps = [t_q, t_consts, psw_state[bk]]
                        if first:
                            tdeps.append(tt_t[it])
                            first = False
                        t_t = p.te(lambda e, i=i, r0=r0, rn=rn, PSX=PSX: [
                            e.transpose(PSX[:, k * 128:(k + 1) * 128],
                                        QN[i][:, (r0 + k) * 128:(r0 + k + 1) * 128],
                                        idn[:])
                            for k in range(rn)][-1], deps=tdeps)

                        def mk_pe(e, it=it, r0=r0, rn=rn, PSX=PSX, bk=bk):
                            dst = TTL[it][:, r0 * 128:(r0 + rn) * 128]
                            if bk == 0:
                                return e.activation(out=dst, in_=PSX[:, 0:rn * 128],
                                                    func=AF.Copy)
                            return e.tensor_copy(out=dst, in_=PSX[:, 0:rn * 128])
                        ev = p.op("act" if bk == 0 else "dve", mk_pe, deps=[t_t])
                        psw_state[bk] = ev
                        evs.append(ev)
                    qn_t[i] = evs
                    t_d = p.dma("pool", lambda e, it=it, rc=rc, cg=cg:
                                e.dma_start(
                                    out=dst_ap[rc * P:(rc + 1) * P,
                                               cg * CG:(cg + 1) * CG],
                                    in_=TTL[it][:, 0:CG]),
                                deps=evs, key="%s%d" % (dkey, it))
                    tt_t[it] = t_d
                    dump_tok[(name, rc, cg)] = t_d
                steps.append(step)
        return steps

    CGD = min(D, 2048)
    dep_c = lambda: t_consts
    r_tok = [None] * T128
    x_steps = []
    for t in range(T128):
        def xstep(t=t):
            sl = ws_ctr[0] % 3
            ws_ctr[0] += 1
            t_l = p.dma("sync", lambda e, sl=sl, t=t:
                        e.dma_start(out=WS[sl][:, 0:D],
                                    in_=x_in[t * P:(t + 1) * P, :]),
                        deps=[t_consts, ws_tok[sl]], key="ws%d" % sl)
            t_c = p.op("dve", lambda e, sl=sl, t=t:
                       e.tensor_copy(out=R[:, t * D:(t + 1) * D],
                                     in_=WS[sl][:, 0:D]),
                       deps=[t_l])
            ws_tok[sl] = t_c
            r_tok[t] = t_c
        x_steps.append(xstep)
    a_steps = (prep_steps("wq", wq_in, AWQ, D, D, CGD, False, 0, dep_c)
               + prep_steps("wk", wk_in, AWK, KVD, D, CGD, False, 0, dep_c)
               + prep_steps("wv", wv_in, AWV, KVD, D, CGD, False, 0, dep_c)
               + prep_steps("wo", wo_in, AWO, D, D, CGD, False, 0, dep_c)
               + x_steps)
    na, ng_ = len(a_steps), len(g_steps)
    ai = 0
    for gi, gs in enumerate(g_steps):
        gs()
        while ai * ng_ < (gi + 1) * na:
            a_steps[ai]()
            ai += 1
    while ai < na:
        a_steps[ai]()
        ai += 1
    t_dumps = [(p.dsem["dump%d" % i], p.dcnt["dump%d" % i]) for i in range(4)]

    t_abs = gp_state["last"]
    t_red = None
    for m in range(3):
        t_red = p.op("dve", lambda e, m=m: e.tensor_reduce(
            out=g3[:, m:m + 1], in_=gpart[:, m * ngch:(m + 1) * ngch],
            axis=AX.X, op=ALU.add), deps=[t_abs])
    t_z = p.op("dve", lambda e: e.memset(g3[:, 3:4], 0.0), deps=[t_red])
    t_gmm = p.te(lambda e: e.matmul(PSG[0:1, 0:4], ones1, g3[:, 0:4],
                                    start=True, stop=True), deps=[t_z, t_consts])
    t_gc = p.op("dve", lambda e: e.tensor_copy(out=gsb[:, :], in_=PSG[0:1, 0:4]),
                deps=[t_gmm])
    t_bc = p.te(lambda e: e.matmul(PSG[:, 0:4], onesr[:, :], gsb[:, :],
                                   start=True, stop=True), deps=[t_gc])
    p.op("act", lambda e: e.activation(
        out=thr[:, 0:4], in_=PSG[:, 0:4], func=AF.Copy,
        scale=0.5 / NW, bias=0.5e-8), deps=[t_bc])
    t_thr = p.op("act", lambda e: e.activation(
        out=thr[:, 4:8], in_=PSG[:, 0:4], func=AF.Copy,
        scale=-0.5 / NW, bias=-0.5e-8), deps=[t_bc])
    if debug:
        p.dma("pool", lambda e: e.dma_start(out=dbgt[:, :], in_=thr[:, :]),
              deps=[t_thr], key="dbg")

    nch = D // 512

    def emit_ln(r_off, out_ap, deps):
        d = list(deps)
        for c in range(nch):
            p.op("dve", lambda e, c=c: e.bn_stats(
                out=stats[:, 6 * c:6 * (c + 1)],
                in_=R[:, r_off + c * 512:r_off + (c + 1) * 512]), deps=d)
            d = []
        t_ag = p.op("dve", lambda e: e.bn_aggr(out=mv[:, :], in_=stats[:, 0:6 * nch]))
        t_s = p.op("act", lambda e: e.activation(
            out=rstd[:, :], in_=mv[:, 1:2], func=AF.Sqrt, bias=eps_ap),
            deps=[t_ag, t_consts])
        p.op("dve", lambda e: e.reciprocal(out=rstd[:, :], in_=rstd[:, :]),
             deps=[t_s])
        t_n = p.op("dve", lambda e: e.tensor_scalar(
            out=out_ap, in0=R[:, r_off:r_off + D],
            scalar1=mv[:, 0:1], scalar2=rstd[:, :],
            op0=ALU.subtract, op1=ALU.mult))
        return t_n

    HFM = ARB[:, 0:DC * TT]
    HFM3 = HFM.rearrange("p (c t) -> p c t", c=DC)
    HIDDEN = ARB[:, DC * TT:DC * TT + FC * TT]
    HSTG = ARB[:, DC * TT + FC * TT:DC * TT + FC * TT + 2048]
    WAR0 = DC * TT + FC * TT + 2048
    W1S = [ARB[:, WAR0 + k * D:WAR0 + (k + 1) * D] for k in range(2)]
    W2S = [ARB[:, WAR0 + 2 * D + k * D:WAR0 + 2 * D + (k + 1) * D] for k in range(2)]
    W3S = [ARB[:, WAR0 + 4 * D + k * F:WAR0 + 4 * D + (k + 1) * F] for k in range(2)]

    def emit_ffn(final, phase_dep, w3dep=None, out_box=None, direct=None,
                 psbox=None):
        w1_tok = [None, None]
        w2_tok = [None, None]
        w3_tok = [None, None]
        if psbox is None:
            psbox = [phase_dep]
        psfq = [phase_dep] * 4
        sig_tok = [None, None]
        hfm_te_tok = None
        hid_war = None
        outs_tok = [phase_dep, phase_dep]
        last_out = None

        for tt in range(NTT):
            t_hfm_ev = []
            for u in range(TSUB):
                t = tt * TSUB + u
                w3w = direct["w3war"]() if (direct and tt >= 1) else None
                t_n = emit_ln(t * D, HSTG[:, 0:D],
                              [r_tok[t], phase_dep, hfm_te_tok, w3w])
                for k0 in range(0, DC, 8):
                    kn = min(8, DC - k0)
                    t_t = p.te(lambda e, k0=k0, kn=kn: [
                        e.transpose(PSB[:, k * 128:(k + 1) * 128],
                                    HSTG[:, (k0 + k) * 128:(k0 + k + 1) * 128],
                                    idn[:]) for k in range(kn)][-1],
                        deps=[t_n, psbox[0], t_consts])
                    hfm_te_tok = t_t
                    t_e = p.op("act", lambda e, k0=k0, kn=kn, u=u: e.activation(
                        out=HFM3[:, k0:k0 + kn, u * 128:(u + 1) * 128],
                        in_=PSB[:, 0:kn * 128], func=AF.Copy), deps=[t_t])
                    psbox[0] = t_e
                    t_hfm_ev.append(t_e)
            t_hid = {}
            for ft in range(FC):
                yield ("ft", tt, ft)
                sl = ft % 2
                if direct is not None and tt == 0:
                    W1A, W2A = W1S[sl], W2S[sl]
                    d1 = list(direct["evs"][("w1", ft)])
                    d2 = list(direct["evs"][("w2", ft)])
                else:
                    if direct is not None and not direct.get("seeded"):
                        w1_tok[0], w1_tok[1] = direct["war1"][0], direct["war1"][1]
                        w2_tok[0], w2_tok[1] = direct["war2"][0], direct["war2"][1]
                        direct["seeded"] = True
                    t_l1 = p.dma("pool", lambda e, sl=sl, ft=ft:
                                 e.dma_start(out=W1S[sl][:, :],
                                             in_=QW1[ft * P:(ft + 1) * P, :]),
                                 deps=[dump_tok[("w1", ft, 0)], w1_tok[sl],
                                       phase_dep], key="w1s%d" % sl)
                    t_l2 = p.dma("pool", lambda e, sl=sl, ft=ft:
                                 e.dma_start(out=W2S[sl][:, :],
                                             in_=QW2[ft * P:(ft + 1) * P, :]),
                                 deps=[dump_tok[("w2", ft, 0)], w2_tok[sl],
                                       phase_dep], key="w2s%d" % sl)
                    W1A, W2A = W1S[sl], W2S[sl]
                    d1, d2 = [t_l1], [t_l2]
                bs = ft % 2
                q1 = bs * 1024
                q2 = bs * 1024 + 512
                t_m1 = p.te(lambda e, W=W1A, q1=q1: [
                    e.matmul(PSF[:, q1 + 0:q1 + TT],
                             W[:, k * 128:(k + 1) * 128],
                             HFM[:, k * TT:(k + 1) * TT],
                             start=(k == 0), stop=(k == DC - 1))
                    for k in range(DC)][-1],
                    deps=t_hfm_ev + d1 + [psfq[2 * bs]])
                t_m2 = p.te(lambda e, W=W2A, q2=q2: [
                    e.matmul(PSF[:, q2:q2 + TT],
                             W[:, k * 128:(k + 1) * 128],
                             HFM[:, k * TT:(k + 1) * TT],
                             start=(k == 0), stop=(k == DC - 1))
                    for k in range(DC)][-1], deps=d2 + [psfq[2 * bs + 1]])
                if direct is not None and tt == 0:
                    direct["feed"](ft, t_m1, t_m2)
                else:
                    w1_tok[sl] = t_m1
                    w2_tok[sl] = t_m2
                hfm_te_tok = t_m2
                t_sg = p.op("act", lambda e, sl=sl, q1=q1: e.activation(
                    out=HSTG[:, sl * 512:sl * 512 + 512], in_=PSF[:, q1:q1 + TT],
                    func=AF.Sigmoid, scale=-1.0), deps=[t_m1, sig_tok[sl]])
                psfq[2 * bs] = t_sg
                t_h = p.op("dve", lambda e, sl=sl, q2=q2, ft=ft: e.tensor_tensor(
                    out=HIDDEN[:, ft * TT:(ft + 1) * TT],
                    in0=HSTG[:, sl * 512:sl * 512 + 512],
                    in1=PSF[:, q2:q2 + TT], op=ALU.mult),
                    deps=[t_sg, t_m2, hid_war])
                psfq[2 * bs + 1] = t_h
                sig_tok[sl] = t_h
                t_hid[ft] = t_h
            t_hid_all = t_hid[FC - 1]
            for dt in range(DC):
                yield ("dt", tt, dt)
                sl = dt % 2
                if direct is not None and tt == 0:
                    d3 = list(direct["w3evs"][dt])
                else:
                    if direct is not None and not direct.get("seeded3"):
                        w3_tok[0] = direct["war3"][0]
                        w3_tok[1] = direct["war3"][1]
                        direct["seeded3"] = True
                    t_l3 = p.dma("pool", lambda e, sl=sl, dt=dt:
                                 e.dma_start(out=W3S[sl][:, :],
                                             in_=QW3[dt * P:(dt + 1) * P, :]),
                                 deps=[dump_tok[("w3row", dt)],
                                       w3_tok[sl], phase_dep,
                                       (w3dep[0] if w3dep else None)],
                                 key="w3s%d" % sl)
                    d3 = [t_l3]
                bs = dt % 4
                q3 = bs * 512
                t_m3 = p.te(lambda e, sl=sl, q3=q3: [
                    e.matmul(PSF[:, q3:q3 + TT],
                             W3S[sl][:, hc * 128:(hc + 1) * 128],
                             HIDDEN[:, hc * TT:(hc + 1) * TT],
                             start=(hc == 0), stop=(hc == FC - 1))
                    for hc in range(FC)][-1],
                    deps=d3 + [t_hid_all, psfq[bs]])
                if direct is not None and tt == 0:
                    direct["w3feed"](dt, t_m3)
                else:
                    w3_tok[sl] = t_m3
                hid_war = t_m3
                xsl = dt % 2
                t_e3 = p.op("act", lambda e, q3=q3, xsl=xsl: e.activation(
                    out=HSTG[:, 1024 + xsl * 512:1536 + xsl * 512],
                    in_=PSF[:, q3:q3 + TT], func=AF.Copy),
                    deps=[t_m3])
                psfq[bs] = t_e3
                t_t = p.te(lambda e, xsl=xsl: [
                    e.transpose(PSB[:, u * 128:(u + 1) * 128],
                                HSTG[:, 1024 + xsl * 512 + u * 128:
                                      1024 + xsl * 512 + (u + 1) * 128],
                                idn[:]) for u in range(TSUB)][-1],
                    deps=[t_e3, psbox[0]])
                hfm_te_tok = t_t
                if not final:
                    t_r = p.op("dve", lambda e, tt=tt, dt=dt:
                               e.scalar_tensor_tensor(
                                   out=R3[:, tt * TSUB:(tt + 1) * TSUB,
                                          dt * 128:(dt + 1) * 128],
                                   in0=PSB[:, 0:TSUB * 128], scalar=0.5,
                                   in1=R3[:, tt * TSUB:(tt + 1) * TSUB,
                                          dt * 128:(dt + 1) * 128],
                                   op0=ALU.mult, op1=ALU.add), deps=[t_t])
                    for u in range(TSUB):
                        r_tok[tt * TSUB + u] = t_r
                    psbox[0] = t_r
                else:
                    osl = dt % 2
                    t_o = p.op("dve", lambda e, tt=tt, dt=dt, osl=osl:
                               e.scalar_tensor_tensor(
                                   out=OUTS[osl][:, 0:TSUB * 128],
                                   in0=PSB[:, 0:TSUB * 128], scalar=0.5,
                                   in1=R3[:, tt * TSUB:(tt + 1) * TSUB,
                                          dt * 128:(dt + 1) * 128],
                                   op0=ALU.mult, op1=ALU.add),
                               deps=[t_t, outs_tok[osl]])
                    t_w = p.dma("sync", lambda e, tt=tt, dt=dt, osl=osl:
                                e.dma_start(
                                    out=Y4[:, tt * TSUB:(tt + 1) * TSUB,
                                           dt * 128:(dt + 1) * 128],
                                    in_=OUTS[osl][:, 0:TSUB * 128]), deps=[t_o],
                                key="outs%d" % osl)
                    outs_tok[osl] = t_w
                    psbox[0] = t_o
                    last_out = t_w
        if out_box is not None:
            out_box[0] = last_out

    GB = WAR0 + 4 * D
    QN_B = [ARB[:, GB + j * CGD:GB + (j + 1) * CGD] for j in range(4)]
    assert 4 * CGD <= 2 * F
    war_w1 = [[], []]
    war_w2 = [[], []]
    evs_direct = {}
    qnb_tok = [None, None, None, None]
    qld_tok = {}
    psbox = [psw_state[1]]

    def emit_pair_lq(ft):
        for (name, src, qeng, qn) in (("w1", w1_in, "dve", (2 * ft) % 4),
                                      ("w2", w2_in, "dve", (2 * ft + 1) % 4)):
            ti = 0 if name == "w1" else 1
            sl = ws_ctr[0] % 3
            ws_ctr[0] += 1
            t_l = p.dma("sync", lambda e, sl=sl, src=src, ft=ft:
                        e.dma_start(out=WS[sl][:, 0:D],
                                    in_=src[ft * P:(ft + 1) * P, :]),
                        deps=[t_thr, ws_tok[sl]], key="ws%d" % sl)
            p.op(qeng, lambda e, qn=qn, sl=sl, ti=ti: e.tensor_scalar(
                out=QN_B[qn][:, 0:D], in0=WS[sl][:, 0:D],
                scalar1=thr[:, ti:ti + 1], scalar2=None,
                op0=ALU.is_ge), deps=[t_l, qnb_tok[qn]])
            t_q = p.op(qeng, lambda e, qn=qn, sl=sl, ti=ti: e.scalar_tensor_tensor(
                out=QN_B[qn][:, 0:D], in0=WS[sl][:, 0:D],
                scalar=thr[:, 4 + ti:5 + ti], in1=QN_B[qn][:, 0:D],
                op0=ALU.is_le, op1=ALU.subtract))
            ws_tok[sl] = t_q
            qld_tok[(name, ft)] = (qn, t_q)

    def emit_pair_tx(ft):
        sl2 = ft % 2
        for (name, dstQW, WSL) in (("w1", QW1, W1S), ("w2", QW2, W2S)):
            war = war_w1 if name == "w1" else war_w2
            qn, t_q = qld_tok.pop((name, ft))
            evs = []
            for ri, r0 in enumerate(range(0, DC, 8)):
                rn = min(8, DC - r0)
                PSX = PSW if ri % 2 == 0 else PSB
                pstok = psw_state if ri % 2 == 0 else psbox
                t_t = p.te(lambda e, qn=qn, r0=r0, rn=rn, PSX=PSX: [
                    e.transpose(PSX[:, k * 128:(k + 1) * 128],
                                QN_B[qn][:, (r0 + k) * 128:(r0 + k + 1) * 128],
                                idn[:]) for k in range(rn)][-1],
                    deps=[t_q, t_consts, pstok[0]])
                ev = p.op("act", lambda e, W=WSL[sl2], r0=r0, rn=rn, PSX=PSX:
                          e.activation(out=W[:, r0 * 128:(r0 + rn) * 128],
                                       in_=PSX[:, 0:rn * 128], func=AF.Copy),
                          deps=[t_t] + war[sl2])
                pstok[0] = ev
                evs.append(ev)
            qnb_tok[qn] = evs
            t_d = p.dma("pool", lambda e, W=WSL[sl2], dstQW=dstQW, ft=ft:
                        e.dma_start(out=dstQW[ft * P:(ft + 1) * P, :],
                                    in_=W[:, 0:D]),
                        deps=evs, key="dp%s%d" % (name, sl2))
            dump_tok[(name, ft, 0)] = t_d
            del war[sl2][:]
            war[sl2].append(t_d)
            evs_direct[(name, ft)] = evs

    def feed(ft, t_m1, t_m2):
        war_w1[ft % 2].append(t_m1)
        war_w2[ft % 2].append(t_m2)

    NC3 = W3CG // P
    QN_W3 = [HFM[:, j * W3CG:(j + 1) * W3CG] for j in range(4)]
    assert 4 * W3CG <= DC * TT
    w3qn_tok = [None, None, None, None]
    war_w3 = [[], []]
    w3evs = {}
    w3lq_tok = {}
    w3_ctr = [0]
    hfm_gate_box = [None]

    def emit_w3row_lq(rc):
        for cg in range(NW3G):
            qn = w3_ctr[0] % 4
            w3_ctr[0] += 1
            sl = ws_ctr[0] % 3
            ws_ctr[0] += 1
            t_l = p.dma("sync", lambda e, sl=sl, rc=rc, cg=cg:
                        e.dma_start(out=WS[sl][:, 0:W3CG],
                                    in_=w3_in[rc * P:(rc + 1) * P,
                                              cg * W3CG:(cg + 1) * W3CG]),
                        deps=[t_thr, ws_tok[sl]], key="ws%d" % sl)
            p.op("dve", lambda e, qn=qn, sl=sl: e.tensor_scalar(
                out=QN_W3[qn][:, 0:W3CG], in0=WS[sl][:, 0:W3CG],
                scalar1=thr[:, 2:3], scalar2=None, op0=ALU.is_ge),
                deps=[t_l, w3qn_tok[qn], hfm_gate_box[0]])
            t_q = p.op("dve", lambda e, qn=qn, sl=sl: e.scalar_tensor_tensor(
                out=QN_W3[qn][:, 0:W3CG], in0=WS[sl][:, 0:W3CG],
                scalar=thr[:, 6:7], in1=QN_W3[qn][:, 0:W3CG],
                op0=ALU.is_le, op1=ALU.subtract))
            ws_tok[sl] = t_q
            w3lq_tok[(rc, cg)] = (qn, t_q)

    def emit_w3row_tx(rc, extra_war=()):
        sl2 = rc % 2
        evs = []
        for cg in range(NW3G):
            qn, t_q = w3lq_tok.pop((rc, cg))
            cevs = []
            for ri, r0 in enumerate(range(0, NC3, 8)):
                rn = min(8, NC3 - r0)
                PSX = PSW if ri % 2 == 0 else PSB
                pstok = psw_state if ri % 2 == 0 else psbox
                t_t = p.te(lambda e, qn=qn, r0=r0, rn=rn, PSX=PSX: [
                    e.transpose(PSX[:, k * 128:(k + 1) * 128],
                                QN_W3[qn][:, (r0 + k) * 128:(r0 + k + 1) * 128],
                                idn[:]) for k in range(rn)][-1],
                    deps=[t_q, pstok[0]])
                ev = p.op("act", lambda e, sl2=sl2, cg=cg, r0=r0, rn=rn,
                          PSX=PSX: e.activation(
                              out=W3S[sl2][:, cg * W3CG + r0 * 128:
                                           cg * W3CG + (r0 + rn) * 128],
                              in_=PSX[:, 0:rn * 128], func=AF.Copy),
                          deps=[t_t] + war_w3[sl2] + list(extra_war))
                pstok[0] = ev
                cevs.append(ev)
            w3qn_tok[qn] = cevs
            evs.extend(cevs)
        t_d = p.dma("pool", lambda e, sl2=sl2, rc=rc:
                    e.dma_start(out=QW3[rc * P:(rc + 1) * P, :],
                                in_=W3S[sl2][:, 0:F]),
                    deps=evs, key="dw3%d" % sl2)
        dump_tok[("w3row", rc)] = t_d
        del war_w3[sl2][:]
        war_w3[sl2].append(t_d)
        w3evs[rc] = evs

    def w3feed(dt, t_m3):
        war_w3[dt % 2].append(t_m3)

    phase_f1 = p.join() + p.dma_join()
    w3dep_box = [None]
    emit_pair_lq(0)
    emit_pair_lq(1)
    emit_pair_tx(0)
    direct_cfg = dict(
        evs=evs_direct, war1=war_w1, war2=war_w2, feed=feed,
        w3evs=w3evs, war3=war_w3, w3feed=w3feed,
        w3war=lambda: [(p.dsem[k], p.dcnt[k]) for k in ("dw30", "dw31")
                       if k in p.dsem])
    for marker in emit_ffn(False, phase_f1, w3dep=w3dep_box, direct=direct_cfg,
                           psbox=psbox):
        kind, tt, idx = marker
        if tt == 0 and kind == "ft":
            if idx + 2 < FC:
                emit_pair_lq(idx + 2)
            if idx + 1 < FC:
                emit_pair_tx(idx + 1)
        elif tt == 0 and kind == "dt":
            if w3dep_box[0] is None:
                w3dep_box[0] = (p.join()
                                + [(p.dsem[k2], p.dcnt[k2]) for k2 in p.dsem
                                   if k2.startswith("dp")])
                hfm_gate_box[0] = (p.sem["te"], p.cnt["te"])
                emit_w3row_lq(0)
                emit_w3row_lq(1)
                emit_w3row_tx(0, extra_war=w3dep_box[0])
            if idx + 2 < DC:
                emit_w3row_lq(idx + 2)
            if idx + 1 < DC:
                emit_w3row_tx(idx + 1,
                              extra_war=(w3dep_box[0] if idx + 1 == 1 else ()))
    if debug:
        p.dma("pool", lambda e: e.dma_start(out=dbg1[:, :], in_=R[:, :]),
              deps=r_tok + p.join(), key="dbg")

    ATTN_SENTINEL = True
    AK = ARB[:, o_ak:o_ak + KC * S]
    AVR = ARB[:, o_av:o_av + T128 * KVD]
    AV3 = AVR.rearrange("p (b v) -> p b v", b=T128)
    SLA = ARB[:, o_sa:o_sa + DC * TT]
    SLA3 = SLA.rearrange("p (c t) -> p c t", c=DC)
    AQ = ARB[:, o_sa:o_sa + NH * TT]
    SLB = ARB[:, o_sb:o_sb + sz_sb]
    SLB_T = ARB[:, o_sb:o_sb + T128 * TT]
    SLB3 = SLB_T.rearrange("p (b t) -> p b t", b=T128)
    SLC = ARB[:, o_sc:o_sc + sz_sc]
    SLC3 = ARB[:, o_sc:o_sc + NH * TT].rearrange("p (c t) -> p c t", c=NH)
    ABUF = ([ARB[:, o_ab + i * S:o_ab + (i + 1) * S] for i in range(2)]
            + [ARB[:, o_hq + D:o_hq + D + S]])
    assert o_hq + D + S <= NBF
    KSTG = [ARB[:, o_ks + i * TT:o_ks + (i + 1) * TT] for i in range(2)]
    HQSTG = ARB[:, o_hq:o_hq + D]

    def emit_attention(phase_dep):
        t_wkl = [p.dma("sync", lambda e, kc=kc: e.dma_start(
            out=SLB[:, kc * D:(kc + 1) * D], in_=AWK[kc * P:(kc + 1) * P, :]),
            deps=[t_dumps, phase_dep], key="awk") for kc in range(KC)]
        t_wkl = [t_wkl[-1]] * KC
        t_wvl = [p.dma("sync", lambda e, kc=kc: e.dma_start(
            out=SLC[:, kc * D:(kc + 1) * D], in_=AWV[kc * P:(kc + 1) * P, :]),
            deps=[t_dumps, phase_dep], key="awv") for kc in range(KC)]
        t_wvl = [t_wvl[-1]] * KC

        psb_tok = phase_dep
        psg0_tok = phase_dep
        psg1_tok = phase_dep
        psf_a_tok = phase_dep
        kstg_tok = [None, None]
        sla_te = None
        stg_te = None
        t_k_ready = [[None] * KC for _ in range(NTT)]
        t_v_ready = [[None] * KC for _ in range(NTT)]
        for tt in range(NTT):
            t_hfm_ev = []
            for u in range(TSUB):
                t = tt * TSUB + u
                t_n = emit_ln(t * D, HQSTG[:, 0:D],
                              [r_tok[t], phase_dep, sla_te, stg_te])
                for k0 in range(0, DC, 8):
                    kn = min(8, DC - k0)
                    t_t = p.te(lambda e, k0=k0, kn=kn: [
                        e.transpose(PSB[:, k * 128:(k + 1) * 128],
                                    HQSTG[:, (k0 + k) * 128:(k0 + k + 1) * 128],
                                    idn[:]) for k in range(kn)][-1],
                        deps=[t_n, psb_tok, t_consts])
                    stg_te = t_t
                    t_e = p.op("act", lambda e, k0=k0, kn=kn, u=u: e.activation(
                        out=SLA3[:, k0:k0 + kn, u * 128:(u + 1) * 128],
                        in_=PSB[:, 0:kn * 128], func=AF.Copy), deps=[t_t])
                    psb_tok = t_e
                    t_hfm_ev.append(t_e)
            pend = [None]

            def k_post(kc, t_mk, tt=tt):
                nonlocal psg0_tok, psg1_tok, psf_a_tok
                sl = kc % 2
                kq = sl * 512
                t_ek = p.op("act", lambda e, sl=sl, kq=kq: e.activation(
                    out=KSTG[sl][:, :], in_=PSG[:, kq:kq + TT], func=AF.Copy),
                    deps=[t_mk, kstg_tok[sl]])
                if sl == 0:
                    psg0_tok = t_ek
                else:
                    psg1_tok = t_ek
                t_pm = p.te(lambda e, sl=sl: e.matmul(
                    PSF[:, 0:TT], pmt[:], KSTG[sl][:, :],
                    start=True, stop=True), deps=[t_ek, psf_a_tok])
                p.op("dve", lambda e, sl=sl, tt=tt: e.tensor_tensor(
                    out=SC1[:, :], in0=KSTG[sl][:, :],
                    in1=ropec[:, tt * TT:(tt + 1) * TT], op=ALU.mult),
                    deps=[t_ek])
                t_t2 = p.op("dve", lambda e, tt=tt: e.tensor_tensor(
                    out=SC2[:, :], in0=PSF[:, 0:TT],
                    in1=ropes[:, tt * TT:(tt + 1) * TT], op=ALU.mult),
                    deps=[t_pm])
                psf_a_tok = t_t2
                t_kf = p.op("dve", lambda e, kc=kc, tt=tt: e.tensor_tensor(
                    out=AK[:, kc * S + tt * TT:kc * S + (tt + 1) * TT],
                    in0=SC1[:, :], in1=SC2[:, :], op=ALU.add))
                kstg_tok[sl] = t_kf
                t_k_ready[tt][kc] = t_kf

            for kc in range(KC):
                sl = kc % 2
                kq = sl * 512
                t_mk = p.te(lambda e, kc=kc, kq=kq: [
                    e.matmul(PSG[:, kq:kq + TT],
                             SLB[:, kc * D + k * 128:kc * D + (k + 1) * 128],
                             SLA[:, k * TT:(k + 1) * TT],
                             start=(k == 0), stop=(k == DC - 1))
                    for k in range(DC)][-1],
                    deps=t_hfm_ev + [t_wkl[kc],
                                     psg0_tok if sl == 0 else psg1_tok])
                sla_te = t_mk
                if pend[0] is not None:
                    pend[0]()
                pend[0] = lambda kc=kc, t_mk=t_mk: k_post(kc, t_mk)
            def v_post(vc, t_mv, tt=tt):
                nonlocal psg0_tok, psg1_tok, psb_tok
                sl = vc % 2
                kq = sl * 512
                t_ev = p.op("act", lambda e, sl=sl, kq=kq: e.activation(
                    out=KSTG[sl][:, :], in_=PSG[:, kq:kq + TT], func=AF.Copy),
                    deps=[t_mv, kstg_tok[sl]])
                if sl == 0:
                    psg0_tok = t_ev
                else:
                    psg1_tok = t_ev
                t_tv = p.te(lambda e, sl=sl: [
                    e.transpose(PSB[:, u * 128:(u + 1) * 128],
                                KSTG[sl][:, u * 128:(u + 1) * 128],
                                idn[:]) for u in range(TSUB)][-1],
                    deps=[t_ev, psb_tok])
                kstg_tok[sl] = t_tv
                t_evv = p.op("act", lambda e, vc=vc, tt=tt: e.activation(
                    out=AV3[:, tt * TSUB:tt * TSUB + TSUB, vc * 128:(vc + 1) * 128],
                    in_=PSB[:, 0:TSUB * 128], func=AF.Copy), deps=[t_tv])
                psb_tok = t_evv
                t_v_ready[tt][vc] = t_evv

            for vc in range(KC):
                sl = vc % 2
                kq = sl * 512
                t_mv = p.te(lambda e, vc=vc, kq=kq: [
                    e.matmul(PSG[:, kq:kq + TT],
                             SLC[:, vc * D + k * 128:vc * D + (k + 1) * 128],
                             SLA[:, k * TT:(k + 1) * TT],
                             start=(k == 0), stop=(k == DC - 1))
                    for k in range(DC)][-1],
                    deps=t_hfm_ev + [t_wvl[vc],
                                     psg0_tok if sl == 0 else psg1_tok])
                sla_te = t_mv
                if pend[0] is not None:
                    pend[0]()
                pend[0] = lambda vc=vc, t_mv=t_mv: v_post(vc, t_mv)
            if pend[0] is not None:
                pend[0]()
                pend[0] = None

        t_kv_join = p.join()
        abuf_tok = [t_kv_join, t_kv_join, t_kv_join]
        psfq4 = [t_kv_join] * 4
        ssb_tok = [t_kv_join, t_kv_join, t_kv_join]
        stats_tok = [t_kv_join, t_kv_join, t_kv_join]
        rs_tok = [t_kv_join, t_kv_join, t_kv_join]
        psx_rr = [0]
        pending_wo = [None]
        psx_tok = [psb_tok, t_kv_join]
        psgh = [psg0_tok, psg1_tok]
        kst2_tok = [t_kv_join, t_kv_join]
        o3_war = t_kv_join
        slb_te = t_kv_join
        stg_te = None
        for qt in range(NTT):
            t_hq_ev = []
            for u in range(TSUB):
                t = qt * TSUB + u
                t_n = emit_ln(t * D, HQSTG[:, 0:D], [r_tok[t], slb_te, stg_te])
                for k0 in range(0, DC, 8):
                    kn = min(8, DC - k0)
                    t_t = p.te(lambda e, k0=k0, kn=kn: [
                        e.transpose(PSB[:, k * 128:(k + 1) * 128],
                                    HQSTG[:, (k0 + k) * 128:(k0 + k + 1) * 128],
                                    idn[:]) for k in range(kn)][-1],
                        deps=[t_n, psx_tok[0]])
                    stg_te = t_t
                    t_e = p.op("act", lambda e, k0=k0, kn=kn, u=u: e.activation(
                        out=SLB3[:, k0:k0 + kn, u * 128:(u + 1) * 128],
                        in_=PSB[:, 0:kn * 128], func=AF.Copy), deps=[t_t])
                    psx_tok[0] = t_e
                    t_hq_ev.append(t_e)
            t_aq = [None] * NH
            qpend = [None]

            def q_post(oc, t_mq, qt=qt):
                sl = oc % 2
                t_eq = p.op("act", lambda e, sl=sl, q=(oc % 2) * 512: e.activation(
                    out=KSTG[sl][:, :], in_=PSG[:, q:q + TT], func=AF.Copy,
                    scale=-INVSQ), deps=[t_mq, kst2_tok[sl]])
                psgh[oc % 2] = t_eq
                t_pm = p.te(lambda e, sl=sl, q=(oc % 2) * 1024: e.matmul(
                    PSF[:, q:q + TT], pmt[:], KSTG[sl][:, :],
                    start=True, stop=True), deps=[t_eq, psfq4[(oc % 2) * 2]])
                p.op("dve", lambda e, sl=sl, qt=qt: e.tensor_tensor(
                    out=SC1[:, :], in0=KSTG[sl][:, :],
                    in1=ropec[:, qt * TT:(qt + 1) * TT], op=ALU.mult),
                    deps=[t_eq])
                t_t2 = p.op("dve", lambda e, q=(oc % 2) * 1024, qt=qt:
                            e.tensor_tensor(
                    out=SC2[:, :], in0=PSF[:, q:q + TT],
                    in1=ropes[:, qt * TT:(qt + 1) * TT], op=ALU.mult),
                    deps=[t_pm])
                psfq4[(oc % 2) * 2] = t_t2
                t_q = p.op("dve", lambda e, oc=oc: e.tensor_tensor(
                    out=AQ[:, oc * TT:(oc + 1) * TT],
                    in0=SC1[:, :], in1=SC2[:, :], op=ALU.add))
                kst2_tok[sl] = t_q
                t_aq[oc] = t_q

            for oc in range(NH):
                sl = oc % 2
                t_lq = p.dma("pool", lambda e, sl=sl, oc=oc: e.dma_start(
                    out=ABUF[sl][:, 0:D], in_=AWQ[oc * P:(oc + 1) * P, :]),
                    deps=[t_dumps, abuf_tok[sl]], key="ab%d" % sl)
                t_mq = p.te(lambda e, sl=sl, q=(oc % 2) * 512: [
                    e.matmul(PSG[:, q:q + TT],
                             ABUF[sl][:, k * 128:(k + 1) * 128],
                             SLB[:, k * TT:(k + 1) * TT],
                             start=(k == 0), stop=(k == DC - 1))
                    for k in range(DC)][-1],
                    deps=t_hq_ev + [t_lq, psgh[oc % 2]])
                abuf_tok[sl] = t_mq
                if qpend[0] is not None:
                    qpend[0]()
                qpend[0] = lambda oc=oc, t_mq=t_mq: q_post(oc, t_mq)
            qpend[0]()
            qpend[0] = None
            pairs = [(h, u) for h in range(NH) for u in range(TSUB)]
            stA = {}
            at_ev = {h: [] for h in range(NH)}
            t_o_ev = None

            def stage_a(idx):
                h, u = pairs[idx]
                g = h // GROUPS
                j = qt * TSUB + u
                klen = (j + 1) * 128
                nkc = (klen + 511) // 512
                par = idx % 3
                t_ev = None
                for kc in range(nkc):
                    n = min(512, klen - kc * 512)
                    diag = (kc == nkc - 1)

                    def mk_sc(e, kc=kc, n=n, diag=diag, h=h, g=g, u=u, klen=klen):
                        i1 = e.matmul(
                            PSF[:, kc * 512:kc * 512 + n],
                            AQ[:, h * TT + u * 128:h * TT + (u + 1) * 128],
                            AK[:, g * S + kc * 512:g * S + kc * 512 + n],
                            start=True, stop=not diag)
                        if diag:
                            i1 = e.matmul(
                                PSF[:, klen - 128:klen],
                                idn[:], tmk[:], start=False, stop=True)
                        return i1
                    t_scc = p.te(mk_sc, deps=[t_aq[h], t_k_ready[qt][g],
                                              t_consts, psfq4[kc]])
                    ev_deps = [t_scc]
                    if kc == 0:
                        ev_deps.append(ssb_tok[par])
                    t_ev = p.op("dve", lambda e, kc=kc, n=n, par=par:
                                e.tensor_scalar(
                                    out=SSB[par][:, kc * 512:kc * 512 + n],
                                    in0=PSF[:, kc * 512:kc * 512 + n],
                                    scalar1=1.0,
                                    scalar2=(None if kc == 0
                                             else mxn2[:, par:par + 1]),
                                    op0=ALU.mult, op1=ALU.min,
                                    accum_out=mxn2[:, par:par + 1]),
                                deps=ev_deps)
                    psfq4[kc] = t_ev
                stA[idx] = (t_ev, klen, par, j)

            def stage_b(idx):
                t_ng, klen, par, j = stA[idx]
                t_ex = p.op("act", lambda e, klen=klen, par=par: e.activation(
                    out=ABUF[par][:, 0:klen], in_=SSB[par][:, 0:klen],
                    func=AF.Exp, scale=-1.0, bias=mxn2[:, par:par + 1],
                    accum_out=sums2[:, par:par + 1]),
                    deps=[t_ng, abuf_tok[par], stats_tok[par]])
                ssb_tok[par] = t_ex
                stA[idx] = (t_ex, klen, par, j)

            def stage_c(idx):
                nonlocal t_o_ev
                h, u = pairs[idx]
                t_ex, klen, par, j = stA[idx]
                del stA[idx]
                t_rc = p.op("dve", lambda e, par=par: e.reciprocal(
                    out=rs2[:, par:par + 1], in_=sums2[:, par:par + 1]),
                    deps=[t_ex, rs_tok[par]])
                stats_tok[par] = t_rc
                t_as = t_ex
                for kb0 in range(0, j + 1, 8):
                    rn = min(8, j + 1 - kb0)
                    t_as = p.op("pool", lambda e, kb0=kb0, rn=rn, par=par:
                                e.tensor_scalar(
                                    out=ABUF[par][:, kb0 * 128:(kb0 + rn) * 128],
                                    in0=ABUF[par][:, kb0 * 128:(kb0 + rn) * 128],
                                    scalar1=rs2[:, par:par + 1], scalar2=1.0,
                                    op0=ALU.mult, op1=ALU.mult),
                                deps=[t_rc, t_ex])
                    rs_tok[par] = t_as
                    bk = psx_rr[0] % 2
                    psx_rr[0] += 1
                    PSX = PSB if bk == 0 else PSW
                    t_tr = p.te(lambda e, par=par, kb0=kb0, rn=rn, PSX=PSX: [
                        e.transpose(PSX[:, k * 128:(k + 1) * 128],
                                    ABUF[par][:, (kb0 + k) * 128:(kb0 + k + 1) * 128],
                                    idn[:]) for k in range(rn)][-1],
                        deps=[t_as, psx_tok[bk]])
                    abuf_tok[par] = t_tr
                    t_ae = p.op("dve", lambda e, kb0=kb0, rn=rn, u=u, PSX=PSX:
                                e.tensor_copy(
                                    out=SLB3[:, kb0:kb0 + rn,
                                             u * 128:(u + 1) * 128],
                                    in_=PSX[:, 0:rn * 128]),
                                deps=[t_tr])
                    psx_tok[bk] = t_ae
                    at_ev[h].append(t_ae)
                if u == TSUB - 1:
                    emit_av(h)

            def emit_av(h):
                nonlocal t_o_ev
                g = h // GROUPS
                avq = (h % 2) * 512

                def mk_av(e, qt=qt, g=g, avq=avq):
                    insts = []
                    first = True
                    for kb in range(qt * TSUB):
                        insts.append(e.matmul(
                            PSG[:, avq:avq + TT],
                            AV3[:, kb, g * 128:(g + 1) * 128],
                            SLB3[:, kb, 0:TT],
                            start=first, stop=False))
                        first = False
                    for v in range(TSUB):
                        kb = qt * TSUB + v
                        insts.append(e.matmul(
                            PSG[:, avq + v * 128:avq + TT],
                            AV3[:, kb, g * 128:(g + 1) * 128],
                            SLB3[:, kb, v * 128:TT],
                            start=first, stop=(v == TSUB - 1)))
                        first = False
                    return insts[-1]
                t_av = p.te(mk_av, deps=at_ev[h]
                            + [t_v_ready[qt][g], psgh[h % 2]])
                t_oe = p.op("act", lambda e, h=h, avq=avq: e.activation(
                    out=SLC3[:, h, 0:TT], in_=PSG[:, avq:avq + TT],
                    func=AF.Copy), deps=[t_av, o3_war])
                psgh[h % 2] = t_oe
                t_o_ev = t_oe
                return t_av

            npair = len(pairs)
            for i in range(npair):
                stage_a(i)
                if i >= 1:
                    stage_b(i - 1)
                if i >= 3:
                    stage_c(i - 3)
            stage_b(npair - 1)
            stage_c(npair - 3)
            stage_c(npair - 2)
            stage_c(npair - 1)
            slb_te = None

            def mk_wo(qt=qt, t_o_all=t_o_ev):
                o3_last = [o3_war]
                wpend = [None]

                def wo_post(oc, t_mo):
                    sl = oc % 2
                    woq = (oc % 2) * 512
                    t_eo = p.op("act", lambda e, sl=sl, woq=woq: e.activation(
                        out=KSTG[sl][:, :], in_=PSG[:, woq:woq + TT],
                        func=AF.Copy), deps=[t_mo, kst2_tok[sl]])
                    psgh[oc % 2] = t_eo
                    PSX = PSB if oc % 2 == 0 else PSW
                    t_to = p.te(lambda e, sl=sl, PSX=PSX: [
                        e.transpose(PSX[:, u * 128:(u + 1) * 128],
                                    KSTG[sl][:, u * 128:(u + 1) * 128],
                                    idn[:]) for u in range(TSUB)][-1],
                        deps=[t_eo, psx_tok[oc % 2]])
                    kst2_tok[sl] = t_to
                    t_r = p.op("dve", lambda e, qt=qt, oc=oc, PSX=PSX:
                               e.scalar_tensor_tensor(
                                   out=R3[:, qt * TSUB:(qt + 1) * TSUB,
                                          oc * 128:(oc + 1) * 128],
                                   in0=PSX[:, 0:TSUB * 128], scalar=1.0,
                                   in1=R3[:, qt * TSUB:(qt + 1) * TSUB,
                                          oc * 128:(oc + 1) * 128],
                                   op0=ALU.mult, op1=ALU.add), deps=[t_to])
                    for u in range(TSUB):
                        r_tok[qt * TSUB + u] = t_r
                    psx_tok[oc % 2] = t_r

                for oc in range(DC):
                    sl = oc % 2
                    t_lo = p.dma("pool", lambda e, sl=sl, oc=oc: e.dma_start(
                        out=ABUF[sl][:, 0:D], in_=AWO[oc * P:(oc + 1) * P, :]),
                        deps=[t_dumps, abuf_tok[sl]], key="ab%d" % sl)
                    t_mo = p.te(lambda e, sl=sl, woq=(oc % 2) * 512: [
                        e.matmul(PSG[:, woq:woq + TT],
                                 ABUF[sl][:, hc * 128:(hc + 1) * 128],
                                 SLC3[:, hc, 0:TT],
                                 start=(hc == 0), stop=(hc == NH - 1))
                        for hc in range(NH)][-1],
                        deps=[t_lo, t_o_all, psgh[oc % 2]])
                    abuf_tok[sl] = t_mo
                    o3_last[0] = t_mo
                    if wpend[0] is not None:
                        wpend[0]()
                    wpend[0] = lambda oc=oc, t_mo=t_mo: wo_post(oc, t_mo)
                wpend[0]()
                return o3_last[0]
            o3_war = mk_wo()

    phase_b = p.join() + p.dma_join()
    emit_attention(phase_b)
    if debug:
        p.dma("pool", lambda e: e.dma_start(out=dbg2[:, :], in_=R[:, :]),
              deps=r_tok + p.join(), key="dbg")


    phase_f2 = p.join() + p.dma_join()
    ob = [None]
    for _ in emit_ffn(True, phase_f2, out_box=ob):
        pass
    last_out = ob[0]

    p.barrier_wait("pool", p.dma_join())
    p.barrier_wait("sync", p.dma_join())

    p.finalize()
    return nc


def make_consts(cfg):
    S, D, HD, theta = cfg["S"], cfg["D"], cfg["HD"], cfg["theta"]
    cid = np.eye(P, dtype=np.float32).astype(BF)
    r = np.arange(P)
    ctm = np.where(r[None, :] > r[:, None], np.float32(NEG),
                   np.float32(0.0)).astype(BF)
    cpm = np.zeros((P, P), dtype=np.float32)
    cpm[r, r ^ 1] = 1.0
    cpm = cpm.astype(BF)
    freqs = (1.0 / (np.float32(theta) ** (np.arange(0, HD, 2, dtype=np.float32) / np.float32(HD)))).astype(np.float32)
    s = np.arange(S, dtype=np.float32)
    ang = s[None, :] * freqs[:, None]
    cos = np.cos(ang).astype(np.float32)
    sin = np.sin(ang).astype(np.float32)
    ropec = np.repeat(cos, 2, axis=0)
    sgn = np.where(np.arange(HD) % 2 == 0, -1.0, 1.0).astype(np.float32)
    ropes = np.repeat(sin, 2, axis=0) * sgn[:, None]
    cmisc = np.zeros((P, 4), dtype=np.float32)
    cmisc[:, 0] = 1e-5
    cmisc[:, 1] = 1.0
    conesr = np.ones((1, P), dtype=np.float32)
    return dict(cid=cid, ctm=ctm, cpm=cpm,
                cropec=np.ascontiguousarray(ropec.astype(BF)),
                cropes=np.ascontiguousarray(ropes.astype(BF)),
                cmisc=cmisc, conesr=conesr)


def make_in_maps(cfg, x_full, Wq, Wk, Wv, Wo, w1, w2, w3):
    ncores = cfg["ncores"]
    SHW = cfg["F"] * cfg["D"] // P
    consts = make_consts(cfg)
    f32 = lambda a: np.ascontiguousarray(np.asarray(a, dtype=np.float32))
    Wq, Wk, Wv, Wo = f32(Wq), f32(Wk), f32(Wv), f32(Wo)
    w1, w2, w3 = f32(w1), f32(w2), f32(w3)
    w1f, w2f, w3f = w1.reshape(-1), w2.reshape(-1), w3.reshape(-1)
    in_maps = []
    for c in range(ncores):
        m = dict(consts)
        m["x"] = f32(x_full[c])
        m["wq"], m["wk"], m["wv"], m["wo"] = Wq, Wk, Wv, Wo
        m["w1"], m["w2"], m["w3"] = w1, w2, w3
        m["gsh1"] = w1f.reshape(P, SHW)
        m["gsh2"] = w2f.reshape(P, SHW)
        m["gsh3"] = w3f.reshape(P, SHW)
        in_maps.append(m)
    return in_maps


_BUILT = {}


def kernel(L0_attn_x, L0_attn_Wq, L0_attn_Wk, L0_attn_Wv, L0_attn_Wo,
           L0_ffn_w1, L0_ffn_w2, L0_ffn_w3):
    cfg = full_cfg()
    key = "full"
    if key not in _BUILT:
        _BUILT[key] = build_program(cfg, debug=False)
    nc = _BUILT[key]
    in_maps = make_in_maps(cfg, np.asarray(L0_attn_x), L0_attn_Wq, L0_attn_Wk,
                           L0_attn_Wv, L0_attn_Wo, L0_ffn_w1, L0_ffn_w2,
                           L0_ffn_w3)
    res = run_bass_kernel_spmd(nc, in_maps, core_ids=list(range(cfg["ncores"])))
    out = np.stack([np.asarray(res.results[i]["y"], dtype=np.float32)
                    for i in range(cfg["ncores"])], axis=0)
    return out
